# version 36
# baseline (speedup 1.0000x reference)
import heapq
import numpy as np
from contextlib import ExitStack

import concourse.bass as bass
import concourse.mybir as mybir
from concourse.bass_utils import run_bass_kernel_spmd

F32 = mybir.dt.float32
BF16 = mybir.dt.bfloat16
AF = mybir.ActivationFunctionType
ALU = mybir.AluOpType
AX = mybir.AxisListType

import os
D = 1024
TOK = 4096
CH = int(os.environ.get("K_CH", "256"))
NCH = TOK // CH
KC = D // 128
SUB = CH // 128
NXIN = int(os.environ.get("K_NXIN", "2"))
NST = int(os.environ.get("K_NST", "4"))
N_P32 = int(os.environ.get("K_P32", "4"))
N_P16 = int(os.environ.get("K_P16", "8"))
N_VOUT = int(os.environ.get("K_VOUT", "1"))
BANK_MODE = os.environ.get("K_BANKS", "global")
PT_ROLE = os.environ.get("K_PTROLE", "0") == "1"
PT1 = os.environ.get("K_PT1", "0") == "1"
RF16 = os.environ.get("K_RF16", "1") == "1"
MSCAN = os.environ.get("K_MSCAN", "0") == "1"
SCAN_G = int(os.environ.get("K_SCANG", "2"))
AS16 = os.environ.get("K_AS16", "0") == "1"
HOPX = float(os.environ.get("K_HOP", "150"))
SQ2 = os.environ.get("K_SQ", "0") == "1"
XFG = os.environ.get("K_XFG", "0") == "1"
WARM_MODEL = os.environ.get("K_WARM", "1") == "1"
TBL_PICK = os.environ.get("K_TBL", "0") == "1"
LN_EPS = 1e-5
RMS_EPS = 1e-6
LOG_CLIP = float(np.log(np.float32(1e-7)))
ENGS = ("tensor", "vector", "scalar", "gpsimd", "sync")
ENGSEM = {"tensor": "T", "vector": "V", "scalar": "S", "gpsimd": "G"}
SELF_CHAIN = ("vector", "scalar", "gpsimd")
HOP_NS = 350


class Op:
    __slots__ = ("idx", "eng", "fn", "deps", "sem", "inc", "est", "inc_val", "afn")

    def __init__(self, idx, eng, fn, deps, sem, inc, est, afn):
        self.idx = idx
        self.eng = eng
        self.fn = fn
        self.deps = deps
        self.sem = sem
        self.inc = inc
        self.est = est
        self.inc_val = None
        self.afn = afn


TBL_NS = 1283.0
TBL_BONUS = float(os.environ.get("K_TBLNS", "2600"))
TBL_WIN = int(os.environ.get("K_TBLWIN", "10"))
N_JIT = int(os.environ.get("K_NJIT", "0"))
JIT_NS = float(os.environ.get("K_JITNS", "900"))
COLD_IDLE = float(os.environ.get("K_CIDLE", "1500"))
COLD_WIN = float(os.environ.get("K_CWIN", "2400"))


class Sched:
    def __init__(self):
        self.all = []
        self.base = {}
        self.dma_val = {}
        self.dma_map = {}

    def emit(self, eng, fn, deps=(), sem=None, inc=0, est=600, afn=None):
        deps = [d for d in deps if d is not None] + self.base.get(eng, [])
        if eng in SELF_CHAIN:
            est += 150.0
        op = Op(len(self.all), eng, fn, list(deps), sem, inc, est, afn)
        self.all.append(op)
        if sem is not None:
            self.dma_val[sem] = self.dma_val.get(sem, 0) + inc
            self.dma_map[(sem, self.dma_val[sem])] = op
        return op

    def schedule(self):
        global HOP_NS
        HOP_NS = HOPX
        n = len(self.all)
        by_sem = {}
        for (s, v), prod in self.dma_map.items():
            by_sem.setdefault(s, []).append((v, prod))
        for s in by_sem:
            by_sem[s].sort(key=lambda t: t[0])
        for op in self.all:
            extra = []
            for d in op.deps:
                if not isinstance(d, Op):
                    s, v = tuple(d)
                    for cv, prod in by_sem.get(s, ()):
                        if cv <= v:
                            extra.append(prod)
                        else:
                            break
            op.deps += extra
        succ = [[] for _ in range(n)]
        npred = [0] * n
        for op in self.all:
            for d in op.deps:
                if isinstance(d, Op):
                    succ[d.idx].append(op.idx)
                    npred[op.idx] += 1
        prio0 = [0.0] * n
        for i in range(n - 1, -1, -1):
            p = 0.0
            for s in succ[i]:
                if prio0[s] > p:
                    p = prio0[s]
            prio0[i] = p + self.all[i].est
        best_res = None
        rng = np.random.RandomState(12345)
        for trial in range(N_JIT + 1):
            if trial == 0:
                prio = prio0
            else:
                prio = [p + float(j) for p, j in
                        zip(prio0, rng.uniform(0.0, JIT_NS, size=n))]
            res = self._sim(n, succ, npred, prio)
            if best_res is None or res[0] < best_res[0]:
                best_res = res
        makespan, streams, finish = best_res
        self.finish = finish
        return streams

    def _sim(self, n, succ, npred, prio):
        eng_free = {e: 0.0 for e in ENGS}
        finish = [0.0] * n
        ready = {e: [] for e in ENGS}
        pend = list(npred)

        def push(i):
            op = self.all[i]
            key = (op.idx, i) if op.eng == "sync" else (-prio[i], i)
            heapq.heappush(ready[op.eng], key)

        for i in range(n):
            if pend[i] == 0:
                push(i)
        streams = {e: [] for e in ENGS}
        remaining = n
        cur_tbl = None
        pe_last_end = -1e18
        pe_cold_until = -1e18

        def pick_scalar():
            tmp = []
            while ready["scalar"] and len(tmp) < TBL_WIN:
                tmp.append(heapq.heappop(ready["scalar"]))
            best_key, best_adj = None, None
            for key in tmp:
                op = self.all[key[-1]]
                adj = -key[0] + (TBL_BONUS if (op.afn and op.afn == cur_tbl) else 0.0)
                if best_adj is None or adj > best_adj:
                    best_adj, best_key = adj, key
            for key in tmp:
                if key is not best_key:
                    heapq.heappush(ready["scalar"], key)
            return best_key

        while remaining:
            best = None
            scalar_key = None
            for e in ENGS:
                if not ready[e]:
                    continue
                if e == "scalar" and TBL_PICK:
                    scalar_key = pick_scalar()
                    i = scalar_key[-1]
                else:
                    i = ready[e][0][-1]
                op = self.all[i]
                dep_t = 0.0
                for d in op.deps:
                    if isinstance(d, Op):
                        t = finish[d.idx]
                        if d.eng != op.eng:
                            t += HOP_NS
                        if t > dep_t:
                            dep_t = t
                start = max(eng_free[e], dep_t)
                if best is None or start < best[0]:
                    best = (start, e, i)
            start, e, i = best
            if e != "scalar":
                if scalar_key is not None:
                    heapq.heappush(ready["scalar"], scalar_key)
                heapq.heappop(ready[e])
            elif scalar_key is None:
                heapq.heappop(ready[e])
            op = self.all[i]
            dur = op.est
            if e == "scalar" and op.afn:
                if op.afn != cur_tbl:
                    dur += TBL_NS
                    cur_tbl = op.afn
            if e == "tensor" and WARM_MODEL:
                if start - pe_last_end > COLD_IDLE:
                    pe_cold_until = start + COLD_WIN
                if start < pe_cold_until:
                    dur = dur * 2.0
                pe_last_end = start + dur
            finish[i] = start + dur
            eng_free[e] = finish[i]
            streams[e].append(op)
            remaining -= 1
            for s in succ[i]:
                pend[s] -= 1
                if pend[s] == 0:
                    push(s)
        return max(finish), streams, finish


class Banks:
    def __init__(self, base, nb):
        self.base = base
        self.nb = nb
        self.free = {}
        self.ptr = 0

    def take(self, n):
        if self.ptr + n > self.nb:
            self.ptr = 0
        ids = [self.base + self.ptr + i for i in range(n)]
        self.ptr = (self.ptr + n) % self.nb
        deps = [t for i in ids for t in self.free.get(i, [])]
        return ids, deps

    def give(self, ids, toks):
        for i in ids:
            self.free[i] = list(toks)


class Pool:
    def __init__(self, slots):
        self.q = [(h, []) for h in slots]

    def take(self):
        best = min(range(len(self.q)),
                   key=lambda i: max([t.idx for t in self.q[i][1]
                                      if isinstance(t, Op)], default=-1))
        return self.q.pop(best)

    def give(self, h, toks):
        self.q.append((h, list(toks)))


def v_tt(f, bf=False):
    return (151 + (f // 2 if bf else f)) / 0.96


def v_ts(f, bf=False):
    return (58 + (f // 4 if bf else f // 2)) / 0.96


def v_red(f):
    return (58 + f) / 0.96


def s_act(f, psum=False):
    return ((330 if psum else 280) + f) / 1.2


def mm_est(nn):
    return nn / 2.4 + (12 if WARM_MODEL else 45)


def build_nc():
    nc = bass.Bass("TRN2", target_bir_lowering=False)

    x_h = nc.dram_tensor("x", [TOK, D], F32, kind="ExternalInput")
    y_h = nc.dram_tensor("y", [TOK, D], F32, kind="ExternalOutput")
    wnames = ("wr", "wi", "ow", "fr", "fk", "fv")
    w_h = {n: nc.dram_tensor(n, [D, D], F32, kind="ExternalInput") for n in wnames}
    convw_h = nc.dram_tensor("convw", [D, 4], F32, kind="ExternalInput")
    vecnames = ("convb", "wrb", "wib", "loga", "mur", "muk")
    v_h = {n: nc.dram_tensor(n, [D], F32, kind="ExternalInput") for n in vecnames}
    normw_h = nc.dram_tensor("normw", [D], F32, kind="ExternalInput")

    sch = Sched()
    if BANK_MODE == "global":
        banksF = banksM = banksB = Banks(0, 7 if PT1 else 6)
    else:
        banksF = Banks(0, 2)
        banksM = Banks(2, 2)
        banksB = Banks(4, 2)

    with ExitStack() as st:
        def sb(name, shape, dt):
            return st.enter_context(nc.sbuf_tensor(name, shape, dt))

        sems = {}
        for sname in (("T", "V", "S", "G", "DC", "DO", "DB") + tuple("DW%d" % i for i in range(6))
                      + tuple("DX%d" % i for i in range(NXIN))):
            sems[sname] = st.enter_context(nc.semaphore(sname))

        _wshrink = os.environ.get("K_WSHRINK", "0") == "1"
        if _wshrink:
            _wshared = sb("w_shared", [128, KC, D], BF16)
            wsb = {n: _wshared for n in wnames}
        else:
            wsb = {n: sb("w_" + n, [128, KC, D], BF16) for n in wnames}
        ident = sb("ident", [128, 128], BF16)
        diag = sb("diag", [128, 32, 128], BF16)
        ones16 = sb("ones16", [128, CH], BF16)
        wconv = sb("wconv", [128, KC, 4], F32)
        cols = {n: sb("c_" + n, [128, KC], F32) for n in vecnames}
        gbias = sb("gbias", [65, D], BF16)
        lnsig = cols["loga"]
        constvals = [1e-5, 1e-6, 1.0, 2.0, -1.0, 1.0 / D]
        cdb = sb("cdb", [128, len(constvals)], F32)
        for i, v in enumerate(constvals):
            nc.const_aps.register(F32, float(v), cdb[:, i:i + 1])
        normw_rep = sb("normw_rep", [128, D], BF16)
        zeros32 = sb("zeros32", [128, CH], F32)
        ln1F = sb("ln1F", [128, KC, CH + 3], BF16)
        fF = sb("fF", [128, KC, CH + 1], BF16)
        scanmask = sb("scanmask", [128, SUB * D], BF16) if MSCAN else None
        carryC = sb("carryC", [128, KC], F32)
        carryS = sb("carryS", [128, KC], F32)
        ctmp = sb("ctmp", [128, KC], F32)
        stmp = sb("stmp", [128, KC], F32)
        stats = sb("stats", [128, NST, 96], F32)
        xin = [sb("xin%d" % i, [128, SUB * D], F32) for i in range(NXIN)]
        vouts = [sb("vout%d" % i, [128, SUB * D], F32) for i in range(N_VOUT)]
        p32 = Pool([sb("p32_%d" % i, [128, SUB * D], F32) for i in range(N_P32)])
        p16 = Pool([sb("p16_%d" % i, [128, SUB * D], BF16) for i in range(N_P16)])
        PS = st.enter_context(nc.psum_tensor("PS", [128, 3584 if PT1 else 3072], F32))
        PT = st.enter_context(nc.psum_tensor("PT", [128, 1024 if PT1 else 2048], BF16))
        tb_free = [[], []]

        def bank(b, n=512):
            return PS[:, b * 512: b * 512 + n]

        def bank2(b):
            return PS[:, b * 512: b * 512 + 1024]

        def tbank(t):
            return PT[:, t * 1024:(t + 1) * 1024]

        def feat(h, c=KC):
            return h[:, :].rearrange("p (c t) -> p c t", c=c)

        def tokv(h):
            return h[:, :].rearrange("p (s d) -> p s d", s=SUB)

        def flat(h):
            return h[:, :]

        def dma(dst, src):
            return lambda eng, o=dst, i=src: eng.dma_start(o, i)

        def dma_nc(dst, src):
            return lambda eng, o=dst, i=src: eng.dma_start(o, i, allow_slow_non_contiguous=True)

        ndc = 0
        for name in ("loga", "mur", "muk", "wrb", "wib"):
            sch.emit("sync", dma_nc(cols[name][:, :], v_h[name][:].rearrange("(k p) -> p k", p=128)),
                     sem="DC", inc=16, est=3000)
            ndc += 1
        sch.emit("sync", dma(wconv[:, :, :], convw_h[:, :].rearrange("(k p) t -> p k t", p=128)),
                 sem="DC", inc=16, est=3000)
        ndc += 1
        sch.emit("gpsimd", dma(normw_rep[0:1, :], normw_h[:].rearrange("(a d) -> a d", a=1)),
                 sem="DB", inc=16, est=3000)
        tok_dc = ("DC", 16 * ndc)
        tok_db = ("DB", 32)
        tok_dcg = tok_dc
        sch.emit("gpsimd", dma(gbias[64:65, :], v_h["convb"][:].rearrange("(a d) -> a d", a=1)),
                 sem="DB", inc=16, est=2000)

        wcast = {}
        for wi_, n in enumerate(wnames):
            wcast[n] = [sch.emit(
                "gpsimd",
                dma(wsb[n][:, :, :], w_h[n][:, :].rearrange("(k p) d -> p k d", p=128)),
                sem="DW%d" % wi_, inc=16, est=12000)]

        def x_chunk_ap(c, h):
            return h[c * CH:(c + 1) * CH, :].rearrange("(s p) d -> p s d", p=128)

        tok_cdb = None
        for i, v in enumerate(constvals):
            tok_cdb = sch.emit("gpsimd", lambda e, i=i, v=v: e.memset(cdb[:, i:i + 1], float(v)),
                               deps=[tok_cdb], est=300)
        sch.base["scalar"] = [tok_cdb]
        tok_i1 = sch.emit("gpsimd", lambda e: e.iota(
            zeros32[:, 0:128], [[1, 128]], channel_multiplier=-1, allow_small_or_imprecise_dtypes=True),
            est=400)
        tok_id = sch.emit("vector", lambda e: e.tensor_single_scalar(ident[:, :], zeros32[:, 0:128], 0.0, ALU.is_equal),
                          deps=[tok_i1], est=300)
        tok_diag = []
        for ch in range(KC):
            for kk in range(4):
                scl = wconv[:, ch, kk:kk + 1]
                tok_diag.append(sch.emit(
                    "scalar",
                    lambda e, i=ch * 4 + kk, scl=scl: e.activation(
                        diag[:, i, :], ident[:, :], AF.Copy,
                        bias=0.0, scale=scl),
                    deps=[tok_id, tok_dc, tok_cdb], est=300))
        tok_ones16 = sch.emit("gpsimd", lambda e: e.memset(ones16[:, :], 1.0), est=300)
        t_sig = sch.emit("scalar", lambda e: e.activation(lnsig[:, :], cols["loga"][:, :], AF.Sigmoid, bias=0.0, scale=1.0),
                         deps=[tok_dc], est=300)
        tok_ln0 = sch.emit("scalar", lambda e: e.activation(lnsig[:, :], lnsig[:, :], AF.Ln, bias=0.0, scale=1.0),
                           deps=[t_sig], est=300)
        tok_lnsig = sch.emit("vector", lambda e: e.tensor_single_scalar(lnsig[:, :], lnsig[:, :], 8.0, ALU.mult),
                             deps=[tok_ln0], est=200)
        tok_nw = []
        for half in range(2):
            ids, bdeps = banksM.take(1)
            b = ids[0]
            tmb = sch.emit(
                "tensor",
                lambda e, b=b, half=half: e.matmul(
                    bank(b), ones16[0:1, 0:128],
                    normw_rep[0:1, half * 512:(half + 1) * 512],
                    start=True, stop=True),
                deps=[tok_ones16, tok_db] + bdeps, est=2000)
            ev = sch.emit(
                "scalar",
                lambda e, b=b, half=half: e.activation(
                    normw_rep[:, half * 512:(half + 1) * 512], bank(b), AF.Copy,
                    bias=0.0, scale=1.0),
                deps=[tmb], est=s_act(512, True))
            tok_nw.append(ev)
            banksM.give(ids, [ev])
        tok_z32 = sch.emit("gpsimd", lambda e: e.memset(zeros32[:, :], 0.0),
                           deps=[tok_id], est=400)
        if MSCAN:
            tok_mask = sch.emit("gpsimd", lambda e: e.memset(scanmask[:, :], 1.0), est=400)
            tok_mask = sch.emit("gpsimd", lambda e: e.memset(feat(scanmask)[:, :, 0], 0.0),
                                deps=[tok_mask], est=300)
        else:
            tok_mask = None
        tok_halo1 = sch.emit("gpsimd", lambda e: e.memset(ln1F[:, :, 0:3], 0.0), est=300)
        tok_cc0 = sch.emit("gpsimd", lambda e: e.memset(carryC[:, :], 0.0), est=200)
        tok_cs0 = sch.emit("gpsimd", lambda e: e.memset(carryS[:, :], 0.0), est=200)
        glob_carry = {"C": tok_cc0, "S": tok_cs0}
        tok_haloF = sch.emit("gpsimd", lambda e: e.memset(fF[:, :, 0:1], 0.0), est=300)

        glob = {
            "nloads": [0] * NXIN, "nstores": 0,
            "xin_free": [[] for _ in range(NXIN)],
            "tok_tail1": tok_halo1, "tok_tailF": tok_haloF,
            "stats_lock": {},
            "vout_lock": [[] for _ in range(N_VOUT)],
        }

        for c in range(min(NXIN, NCH)):
            par = c % NXIN
            glob["nloads"][par] += 1
            sch.emit("sync", dma(tokv(xin[par]), x_chunk_ap(c, x_h)),
                     deps=glob["xin_free"][par], sem="DX%d" % par, inc=16,
                     est=5000)

        def ln_stats(xap, par, base, src_deps, tag, s1_tok=None, lock=None, bgrp=None,
                     sub_deps=None):
            banks = bgrp
            S_ = stats[:, par, :]
            if lock is None:
                lock = glob["stats_lock"].pop((par, tag), [])
            if s1_tok is None:
                t1 = [sch.emit("vector", lambda e, S_=S_, xap=xap, base=base: e.tensor_reduce(
                    S_[:, base:base + SUB], xap, AX.X, ALU.add),
                    deps=src_deps + lock, est=v_red(SUB * D))]
            else:
                t1 = list(s1_tok)
            ids, bdeps = banks.take(2)
            b = ids[0]
            tsq = []
            for s in range(SUB):
                sd = sub_deps[s] if sub_deps is not None else src_deps
                tsq.append(sch.emit(
                    "scalar",
                    lambda e, s2=s, b=b, S_=S_, xap=xap, base=base: e.activation(
                        bank2(b), xap[:, s2, :], AF.Square, bias=0.0, scale=1.0,
                        accum_out=S_[:, base + 2 + s2:base + 3 + s2]),
                    deps=sd + bdeps + lock, est=s_act(D)))
            banks.give(ids, tsq)
            tm_ = sch.emit("vector", lambda e, S_=S_, base=base: e.tensor_single_scalar(
                S_[:, base + 4:base + 4 + SUB], S_[:, base:base + SUB], 1.0 / D, ALU.mult),
                deps=t1, est=150)
            tmm = sch.emit("vector", lambda e, S_=S_, base=base: e.tensor_tensor(
                S_[:, base + 8:base + 8 + SUB], S_[:, base + 4:base + 4 + SUB], S_[:, base + 4:base + 4 + SUB], ALU.mult),
                deps=[tm_], est=150)
            tq = sch.emit("vector", lambda e, S_=S_, base=base: e.scalar_tensor_tensor(
                S_[:, base + 8:base + 8 + SUB], S_[:, base + 2:base + 2 + SUB], 1.0 / D,
                S_[:, base + 8:base + 8 + SUB], ALU.mult, ALU.subtract),
                deps=tsq + [tmm], est=150)
            tln = sch.emit("scalar", lambda e, S_=S_, base=base: e.activation(
                S_[:, base + 14:base + 14 + SUB], S_[:, base + 8:base + 8 + SUB], AF.Ln, bias=LN_EPS, scale=1.0),
                deps=[tq], est=250, afn="exp")
            trs = sch.emit("scalar", lambda e, S_=S_, base=base: e.activation(
                S_[:, base + 10:base + 10 + SUB], S_[:, base + 14:base + 14 + SUB], AF.Exp, bias=0.0, scale=-0.5),
                deps=[tln], est=250, afn="exp")
            return trs, S_, t1 + tsq

        def emit_F_head(c):
            par = c % NXIN
            ps = c % NST
            xb = xin[par]
            xt = tokv(xb)
            dxs = "DX%d" % par
            tok_x = (dxs, 16 * glob["nloads"][par])
            stt = {"par": par, "ps": ps, "xb": xb, "xt": xt, "tok_x": tok_x}

            trs1, S_, rd1 = ln_stats(xt, ps, 0, [tok_x], "a", bgrp=banksF)
            ln1, ln1dep = p16.take()
            tok_ln1 = []
            for s in range(SUB):
                tok_ln1.append(sch.emit(
                    "vector",
                    lambda e, s2=s, S_=S_, ln1=ln1, xt=xt: e.tensor_scalar(
                        tokv(ln1)[:, s2, :], xt[:, s2, :],
                        S_[:, 4 + s2:5 + s2], S_[:, 10 + s2:11 + s2],
                        ALU.subtract, ALU.mult),
                    deps=[trs1, tok_x] + ln1dep, est=v_ts(D)))
            glob["stats_lock"][(ps, "a")] = tok_ln1
            stt["xb_readers"] = rd1 + tok_ln1

            tok_tr1 = []
            tok_ev1 = []
            for s2 in range(SUB):
                tb = 0 if (PT1 or PT_ROLE) else s2
                trs_ = []
                for j in range(KC):
                    trs_.append(sch.emit(
                        "tensor",
                        lambda e, tb=tb, j=j, s2=s2, ln1=ln1: e.transpose(
                            tbank(tb)[:, j * 128:(j + 1) * 128],
                            tokv(ln1)[:, s2, j * 128:(j + 1) * 128],
                            ident[:, :]),
                        deps=[tok_ln1[s2], tok_id] + tb_free[tb], est=65))
                tev = sch.emit(
                    "scalar",
                    lambda e, tb=tb, s2=s2: e.activation(
                        ln1F[:, 0:KC, 3 + s2 * 128: 3 + (s2 + 1) * 128],
                        tbank(tb), AF.Copy, bias=0.0, scale=1.0),
                    deps=trs_ + [glob["tok_tail1"]], est=s_act(D, True))
                tok_ev1.append(tev)
                tok_tr1 += trs_
                tb_free[tb] = [tev]
            p16.give(ln1, tok_tr1)
            stt["tok_ev1"] = tok_ev1
            return stt

        def emit_F_rest(c, stt):
            tok_ev1 = stt["tok_ev1"]
            xcFh, xcFdep = p16.take()
            tok_xcF = []
            tmm_pair = []
            for pr in range(KC // 2):
                ids, bdeps = banksF.take(1)
                b = ids[0]
                tm = None
                for half in range(2):
                    ch = pr * 2 + half
                    for kk in range(4):
                        prev = [tm] if tm is not None else []
                        tm = sch.emit(
                            "tensor",
                            lambda e, b=b, ch=ch, kk=kk, half=half: e.matmul(
                                PS[:, b * 512 + half * CH: b * 512 + half * CH + CH],
                                diag[:, ch * 4 + kk, :], ln1F[:, ch, kk:kk + CH],
                                start=(kk == 0), stop=False),
                            deps=tok_ev1 + [tok_diag[ch * 4 + kk]] + bdeps + prev
                                 + [glob["tok_tail1"]],
                            est=mm_est(CH))
                    tm = sch.emit(
                        "tensor",
                        lambda e, b=b, ch=ch, half=half: e.matmul(
                            PS[:, b * 512 + half * CH: b * 512 + half * CH + CH],
                            gbias[64:65, ch * 128:(ch + 1) * 128], ones16[64:65, :],
                            start=False, stop=True),
                        deps=[tm, tok_db, tok_ones16], est=mm_est(CH))
                tmm_pair.append(tm)
                t2 = sch.emit(
                    "scalar",
                    lambda e, b=b, pr=pr: e.activation(
                        feat(xcFh)[:, pr * 2:pr * 2 + 2, :], bank(b, 2 * CH), AF.Copy,
                        bias=0.0, scale=1.0),
                    deps=[tm] + (xcFdep if pr == 0 else []), est=s_act(512, True))
                tok_xcF.append(t2)
                banksF.give([b], [t2])
            xcF = xcFh
            glob["tok_tail1"] = sch.emit(
                "gpsimd", lambda e: e.tensor_copy(ln1F[:, :, 0:3], ln1F[:, :, CH:CH + 3]),
                deps=tmm_pair + [glob["tok_tail1"]], est=500)

            r16, r16dep = p16.take()
            i16, i16dep = p16.take()
            all_gate_mm = []
            for wn, biasn, gb, gdep in (("wr", "wrb", r16, r16dep),
                                        ("wi", "wib", i16, i16dep)):
                toks = []
                for m in range(KC):
                    ids, bdeps = banksF.take(1)
                    b = ids[0]
                    tmm = None
                    for k in range(KC):
                        prev = [tmm] if tmm is not None else []
                        tmm = sch.emit(
                            "tensor",
                            lambda e, b=b, wn=wn, k=k, m=m, xcF=xcF: e.matmul(
                                bank(b, CH), wsb[wn][:, k, m * 128:(m + 1) * 128],
                                feat(xcF)[:, k, :],
                                start=(k == 0), stop=(k == KC - 1)),
                            deps=tok_xcF + wcast[wn] + bdeps + prev,
                            est=mm_est(CH))
                    all_gate_mm.append(tmm)
                    te = sch.emit(
                        "scalar",
                        lambda e, b=b, gb=gb, m=m, biasn=biasn: e.activation(
                            feat(gb)[:, m, :], bank(b, CH), AF.Sigmoid,
                            bias=cols[biasn][:, m:m + 1], scale=1.0),
                        deps=[tmm, tok_dcg] + (gdep if m == 0 else []),
                        est=s_act(CH, True), afn="sigmoid")
                    toks.append(te)
                    banksF.give(ids, [te])
                stt[wn + "_toks"] = toks
            stt["xcF"] = xcF
            stt["tok_xcF"] = tok_xcF
            stt["r16"] = r16
            stt["i16"] = i16
            stt["gate_mm"] = all_gate_mm

        NSPL = int(os.environ.get("K_NSPL", "1"))
        NHF = SUB * D // NSPL
        CPS = KC // NSPL

        def half(h, j):
            return h[:, j * NHF:(j + 1) * NHF]

        def emit_F_tail(c, stt):
            z, zdep = p32.take()
            tok_z = []
            for j in range(NSPL):
                bc = bass.AP(lnsig, CPS * j, [[KC, 128], [1, CPS], [0, CH]])
                tok_z.append(sch.emit(
                    "vector",
                    lambda e, j=j, z=z, r16=stt["r16"], bc=bc: e.tensor_tensor(
                        feat(half(z, j), CPS), feat(half(r16, j), CPS), bc, ALU.mult),
                    deps=stt["wr_toks"][CPS * j:CPS * (j + 1)] + [tok_lnsig]
                         + (zdep if j == 0 else []), est=v_tt(NHF, True)))
            p16.give(stt["r16"], tok_z)
            tbuf, tdep = p16.take()
            tok_t = []
            for j in range(NSPL):
                tok_t.append(sch.emit(
                    "vector",
                    lambda e, j=j, t=tbuf, i16=stt["i16"], xcF=stt["xcF"]: e.tensor_tensor(
                        half(t, j), half(i16, j), half(xcF, j), ALU.mult),
                    deps=stt["wi_toks"][CPS * j:CPS * (j + 1)] + stt["tok_xcF"][(CPS // 2) * j:(CPS // 2) * (j + 1)]
                         + (tdep if j == 0 else []), est=v_tt(NHF, True)))
            p16.give(stt["i16"], tok_t)
            p16.give(stt["xcF"], tok_t + stt["gate_mm"])
            stt["z"] = z
            stt["tok_z"] = tok_z
            stt["t"] = tbuf
            stt["tok_t"] = tok_t

        def emit_B1(c, stt):
            z, tbuf = stt["z"], stt["t"]

            la, ladep = p32.take()
            tok_la = []
            for j in range(NSPL):
                tok_la.append(sch.emit(
                    "vector", lambda e, j=j, la=la, z=z: e.tensor_single_scalar(
                        half(la, j), half(z, j), LOG_CLIP, ALU.max),
                    deps=[stt["tok_z"][j]] + (ladep if j == 0 else []), est=v_ts(NHF)))
            sbuf_s, sdep = p32.take()
            tok_s = []
            for j in range(NSPL):
                tok_s.append(sch.emit(
                    "scalar", lambda e, j=j, s=sbuf_s, z=z: e.activation(
                        half(s, j), half(z, j), AF.Exp, bias=0.0, scale=2.0),
                    deps=[stt["tok_z"][j]] + (sdep if j == 0 else []),
                    est=s_act(NHF), afn="exp"))
            p32.give(z, tok_s + tok_la)
            tok_sp = [sch.emit(
                "vector", lambda e, j=j, s=sbuf_s: e.tensor_single_scalar(
                    half(s, j), half(s, j), 1.0 - 1e-6, ALU.min),
                deps=[tok_s[j]], est=v_ts(NHF)) for j in range(NSPL)]

            sq16, sqdep16 = p16.take()
            tok_sq = []
            for j in range(NSPL):
                if SQ2:
                    tok_lnw = sch.emit("scalar", lambda e, j=j, s=sbuf_s: e.activation(
                        half(s, j), half(s, j), AF.Ln, bias=1.0, scale=-1.0),
                        deps=[tok_sp[j]], est=s_act(NHF), afn="exp")
                    tok_sq.append(sch.emit(
                        "scalar", lambda e, j=j, s=sbuf_s, sq=sq16: e.activation(
                            half(sq, j), half(s, j), AF.Exp, bias=0.0, scale=0.5),
                        deps=[tok_lnw] + (sqdep16 if j == 0 else []),
                        est=s_act(NHF), afn="exp"))
                else:
                    tok_sq.append(sch.emit(
                        "scalar", lambda e, j=j, s=sbuf_s, sq=sq16: e.activation(
                            half(sq, j), half(s, j), AF.Sqrt, bias=1.0, scale=-1.0),
                        deps=[tok_sp[j]] + (sqdep16 if j == 0 else []),
                        est=s_act(NHF), afn="sqrt"))
            p32.give(sbuf_s, tok_sq)
            tok_u1 = [sch.emit(
                "vector", lambda e, j=j, t=tbuf, sq=sq16: e.tensor_tensor(
                    half(t, j), half(sq, j), half(t, j), ALU.mult),
                deps=[tok_sq[j], stt["tok_t"][j]], est=v_tt(NHF, True))
                for j in range(NSPL)]
            p16.give(sq16, tok_u1)
            cbuf, cdep = p32.take()
            if MSCAN:
                W = SCAN_G * CH
                tok_c = [sch.emit(
                    "vector",
                    lambda e, j=j, cb=cbuf, la=la: e.tensor_tensor_scan(
                        flat(cb)[:, j * W:(j + 1) * W], flat(scanmask)[:, j * W:(j + 1) * W],
                        flat(la)[:, j * W:(j + 1) * W],
                        0.0, ALU.mult, ALU.add),
                    deps=[tok_mask] + tok_la + cdep, est=(400 + W) / 0.96)
                    for j in range(KC // SCAN_G)]
            else:
                tok_c = []
                for m in range(KC):
                    tok_c.append(sch.emit(
                        "vector",
                        lambda e, m=m, cb=cbuf, la=la: e.tensor_tensor_scan(
                            feat(cb)[:, m, :], feat(la)[:, m, :], zeros32[:, :],
                            0.0, ALU.add, ALU.add),
                        deps=[tok_z32] + tok_la + cdep, est=(400 + CH) / 0.96))
            p32.give(la, tok_c)
            prevC = glob_carry["C"]
            tlast = sch.emit("vector", lambda e, cb=cbuf: e.tensor_copy(
                ctmp[:, :], feat(cb)[:, :, CH - 1]), deps=tok_c, est=150)
            tok_m_l = []
            for m in range(KC):
                tok_m_l.append(sch.emit(
                    "vector",
                    lambda e, m=m, cb=cbuf: e.tensor_scalar(
                        feat(cb)[:, m, :], feat(cb)[:, m, :],
                        carryC[:, m:m + 1], -80.0, ALU.add, ALU.max),
                    deps=[tlast, prevC] + tok_c, est=v_ts(CH)))
            glob_carry["C"] = sch.emit(
                "vector", lambda e: e.tensor_tensor(
                    carryC[:, :], carryC[:, :], ctmp[:, :], ALU.add),
                deps=[tlast, prevC] + tok_m_l, est=150)
            tok_m = tok_m_l[-1]
            Ainv, Aidep = p16.take()
            tok_Ai = [sch.emit("scalar", lambda e, j=j, A=Ainv, cb=cbuf: e.activation(
                half(A, j), half(cb, j), AF.Exp, bias=0.0, scale=-1.0),
                deps=tok_m_l[CPS * j:CPS * (j + 1)] + (Aidep if j == 0 else []),
                est=s_act(NHF), afn="exp") for j in range(NSPL)]
            tok_u = [sch.emit("vector", lambda e, j=j, t=tbuf, A=Ainv: e.tensor_tensor(
                half(t, j), half(t, j), half(A, j), ALU.mult),
                deps=[tok_Ai[j], tok_u1[j]], est=v_tt(NHF, True)) for j in range(NSPL)]
            p16.give(Ainv, tok_u)
            Ssc, Sdep = (p16.take() if AS16 else p32.take())
            if MSCAN:
                W = SCAN_G * CH
                tok_S = [sch.emit(
                    "vector",
                    lambda e, j=j, S2=Ssc, t=tbuf: e.tensor_tensor_scan(
                        flat(S2)[:, j * W:(j + 1) * W], flat(scanmask)[:, j * W:(j + 1) * W],
                        flat(t)[:, j * W:(j + 1) * W],
                        0.0, ALU.mult, ALU.add),
                    deps=[tok_mask] + tok_u + Sdep, est=(400 + W) / 0.96)
                    for j in range(KC // SCAN_G)]
            else:
                tok_S = []
                for m in range(KC):
                    tok_S.append(sch.emit(
                        "vector",
                        lambda e, m=m, S2=Ssc, t=tbuf: e.tensor_tensor_scan(
                            feat(S2)[:, m, :], feat(t)[:, m, :], zeros32[:, :],
                            0.0, ALU.add, ALU.add),
                        deps=[tok_z32] + tok_u + Sdep, est=(400 + CH // 2) / 0.96))
            p16.give(tbuf, tok_S)
            prevS = glob_carry["S"]
            slast = sch.emit("vector", lambda e, S2=Ssc: e.tensor_copy(
                stmp[:, :], feat(S2)[:, :, CH - 1]), deps=tok_S, est=150)
            Abuf, Adep = (p16.take() if AS16 else p32.take())
            tok_A = [sch.emit("scalar", lambda e, j=j, A=Abuf, cb=cbuf: e.activation(
                half(A, j), half(cb, j), AF.Exp, bias=0.0, scale=1.0),
                deps=tok_m_l[CPS * j:CPS * (j + 1)] + (Adep if j == 0 else []),
                est=s_act(NHF), afn="exp") for j in range(NSPL)]
            p32.give(cbuf, tok_A + tok_Ai)
            hF, hdep = p16.take()
            tok_h_l = []
            for m in range(KC):
                tok_h_l.append(sch.emit(
                    "vector",
                    lambda e, m=m, h=hF, A=Abuf, S2=Ssc: e.scalar_tensor_tensor(
                        feat(h)[:, m, :], feat(S2)[:, m, :], carryS[:, m:m + 1],
                        feat(A)[:, m, :], ALU.add, ALU.mult),
                    deps=tok_A + [prevS] + tok_S + hdep,
                    est=(151 + (CH // 2 if AS16 else CH)) / 0.96))
            glob_carry["S"] = sch.emit(
                "vector", lambda e: e.tensor_tensor(
                    carryS[:, :], carryS[:, :], stmp[:, :], ALU.add),
                deps=[slast, prevS] + tok_h_l, est=150)
            tok_h = tok_h_l[-1]
            (p16 if AS16 else p32).give(Abuf, tok_h_l)
            (p16 if AS16 else p32).give(Ssc, tok_h_l)
            stt["hF"] = hF
            stt["tok_h"] = tok_h_l

        def emit_B2a(c, stt):
            par = stt["par"]
            ps = stt["ps"]
            xb = stt["xb"]
            xt = stt["xt"]
            hF = stt["hF"]
            tok_h_l = stt["tok_h"]

            vout = vouts[c % N_VOUT]
            stt["vout"] = vout
            o2, o2dep = p32.take()
            S_ = stats[:, ps, :]
            rmslock = glob["stats_lock"].pop((ps, "rms%d" % (c % N_VOUT)), [])
            lock2 = glob["stats_lock"].pop((ps, "b"), [])
            tok_o2 = []
            all_ow = []
            for s in range(SUB):
                ids, bdeps = banksM.take(2)
                tok_half = []
                for half in range(2):
                    b = ids[half]
                    tmm = None
                    for k in range(KC):
                        prev = [tmm] if tmm is not None else []
                        tmm = sch.emit(
                            "tensor",
                            lambda e, b=b, k=k, s=s, half=half, hF=hF: e.matmul(
                                bank(b),
                                feat(hF)[:, k, s * 128:(s + 1) * 128],
                                wsb["ow"][:, k, half * 512:(half + 1) * 512],
                                start=(k == 0), stop=(k == KC - 1)),
                            deps=tok_h_l + wcast["ow"] + bdeps + prev, est=mm_est(512))
                    tok_half.append(tmm)
                all_ow += tok_half
                tsq = []
                for half in range(2):
                    tsq.append(sch.emit(
                        "scalar",
                        lambda e, b=ids[half], s=s, half=half, S_=S_, vout=vout: e.activation(
                            tokv(vout)[:, s, half * 512:(half + 1) * 512], bank(b),
                            AF.Square, bias=0.0, scale=1.0,
                            accum_out=S_[:, 80 + s * 2 + half:81 + s * 2 + half]),
                        deps=[tok_half[half]] + rmslock + glob["vout_lock"][c % N_VOUT]
                             + ([tsq[-1]] if tsq else []),
                        est=s_act(512, True)))
                glob.setdefault("vout_scr%d" % (c % N_VOUT), [])
                glob["vout_scr%d" % (c % N_VOUT)] += tsq
                tadd = sch.emit("vector", lambda e, s=s, S_=S_: e.tensor_tensor(
                    S_[:, 84 + s:85 + s], S_[:, 80 + s * 2:81 + s * 2],
                    S_[:, 81 + s * 2:82 + s * 2], ALU.add), deps=tsq, est=150)
                tlnr = sch.emit("scalar", lambda e, s=s, S_=S_: e.activation(
                    S_[:, 88 + s:89 + s], S_[:, 84 + s:85 + s], AF.Ln,
                    bias=RMS_EPS, scale=1.0 / D), deps=[tadd], est=250, afn="exp")
                trms = sch.emit("scalar", lambda e, s=s, S_=S_: e.activation(
                    S_[:, 86 + s:87 + s], S_[:, 88 + s:89 + s], AF.Exp, bias=0.0, scale=-0.5),
                    deps=[tlnr], est=250, afn="exp")
                evs = []
                for half in range(2):
                    evs.append(sch.emit(
                        "vector",
                        lambda e, b=ids[half], s=s, half=half, o2=o2, S_=S_: e.scalar_tensor_tensor(
                            tokv(o2)[:, s, half * 512:(half + 1) * 512], bank(b),
                            S_[:, 86 + s:87 + s], normw_rep[:, half * 512:(half + 1) * 512],
                            ALU.mult, ALU.mult,
                            accum_out=S_[:, 60 + s * 2 + half:61 + s * 2 + half]),
                        deps=[trms, tok_half[half]] + tok_nw + lock2 + (o2dep if s == 0 and half == 0 else []),
                        est=(120 + 512) / 0.96))
                tok_o2 += evs
                glob["stats_lock"][(ps, "rms%d" % (c % N_VOUT))] = [tlnr, trms] + evs
                banksM.give(ids, tsq + evs)
            p16.give(hF, all_ow)

            acc2 = stats[:, ps, 60:60 + 2 * SUB].rearrange("p (a b) -> p a b", b=2)
            t1b = sch.emit("vector", lambda e, S_=S_, acc2=acc2: e.tensor_tensor(
                S_[:, 16:16 + SUB], acc2[:, :, 0], acc2[:, :, 1], ALU.add),
                deps=tok_o2, est=150)
            trs2, S_, rd2 = ln_stats(tokv(o2), ps, 16, tok_o2, "b", s1_tok=[t1b], lock=lock2, bgrp=banksM,
                                     sub_deps=[tok_o2[0:2], tok_o2[2:4]])
            tok_n2 = []
            for s in range(SUB):
                tok_n2.append(sch.emit(
                    "vector",
                    lambda e, s=s, o2=o2, S_=S_: e.tensor_scalar(
                        tokv(o2)[:, s, :], tokv(o2)[:, s, :],
                        S_[:, 20 + s:21 + s], S_[:, 26 + s:27 + s],
                        ALU.subtract, ALU.mult),
                    deps=[trs2] + rd2, est=v_ts(D)))
            glob["stats_lock"][(ps, "b")] = tok_n2 + [t1b]
            lock3 = glob["stats_lock"].pop((ps, "c"), [])
            tok_x2l = []
            for s in range(SUB):
                tok_x2l.append(sch.emit(
                    "vector",
                    lambda e, s=s, xb=xb, o2=o2, S_=S_: e.scalar_tensor_tensor(
                        tokv(xb)[:, s, :], tokv(o2)[:, s, :], 1.0, tokv(xb)[:, s, :],
                        ALU.mult, ALU.add,
                        accum_out=S_[:, 32 + s:33 + s]),
                    deps=tok_n2 + stt["xb_readers"] + lock3, est=v_tt(1024)))
            tok_x2 = tok_x2l[-1]
            p32.give(o2, tok_x2l)
            stt["tok_x2"] = tok_x2l

            trs3, S_, rd3 = ln_stats(xt, ps, 32, tok_x2l, "c", s1_tok=tok_x2l, lock=lock3, bgrp=banksM,
                                     sub_deps=[[tok_x2l[0]], [tok_x2l[1]]])
            ln3, ln3dep = p16.take()
            tok_ln3 = []
            for s in range(SUB):
                tok_ln3.append(sch.emit(
                    "vector",
                    lambda e, s2=s, ln3=ln3, S_=S_, xt=xt: e.tensor_scalar(
                        tokv(ln3)[:, s2, :], xt[:, s2, :],
                        S_[:, 36 + s2:37 + s2], S_[:, 42 + s2:43 + s2],
                        ALU.subtract, ALU.mult),
                    deps=[trs3] + stt["tok_x2"] + ln3dep, est=v_ts(D)))
            glob["stats_lock"][(ps, "c")] = tok_ln3
            stt["x2_readers"] = rd3 + tok_ln3

            tok_evF = []
            tok_trF = []
            for s2 in range(SUB):
                tb = 0 if PT1 else (1 if PT_ROLE else s2)
                trs_ = []
                for j in range(KC):
                    trs_.append(sch.emit(
                        "tensor",
                        lambda e, tb=tb, j=j, s2=s2, ln3=ln3: e.transpose(
                            tbank(tb)[:, j * 128:(j + 1) * 128],
                            tokv(ln3)[:, s2, j * 128:(j + 1) * 128],
                            ident[:, :]),
                        deps=[tok_ln3[s2], tok_id] + tb_free[tb], est=65))
                tev = sch.emit(
                    "scalar",
                    lambda e, tb=tb, s2=s2: e.activation(
                        fF[:, 0:KC, 1 + s2 * 128: 1 + (s2 + 1) * 128],
                        tbank(tb), AF.Copy, bias=0.0, scale=1.0),
                    deps=trs_ + [glob["tok_tailF"]], est=s_act(D, True))
                tok_evF.append(tev)
                tok_trF += trs_
                tb_free[tb] = [tev]
            p16.give(ln3, tok_trF)
            stt["tok_evF"] = tok_evF

        def emit_B2b(c, stt):
            par = stt["par"]
            ps = stt["ps"]
            xb = stt["xb"]
            tok_evF = stt["tok_evF"]

            dbuf, ddep = p16.take()
            tok_d = sch.emit(
                "vector",
                lambda e, d=dbuf: e.tensor_tensor(
                    feat(d), fF[:, :, 1:CH + 1], fF[:, :, 0:CH], ALU.subtract),
                deps=tok_evF + [glob["tok_tailF"]] + ddep, est=v_tt(SUB * D, True))
            xrF, xrdep = p16.take()
            xkF, xkdep = p16.take()
            mur_bc = bass.AP(cols["mur"], 0, [[KC, 128], [1, KC], [0, CH]])
            muk_bc = bass.AP(cols["muk"], 0, [[KC, 128], [1, KC], [0, CH]])
            t_xr1 = sch.emit("vector", lambda e, d=dbuf, xr=xrF, bc=mur_bc: e.tensor_tensor(
                feat(xr), feat(d), bc, ALU.mult),
                deps=[tok_d, tok_dc] + xrdep, est=v_tt(SUB * D, True))
            tok_xr = [sch.emit("vector", lambda e, xr=xrF: e.tensor_tensor(
                feat(xr), feat(xr), fF[:, :, 0:CH], ALU.add),
                deps=[t_xr1], est=v_tt(SUB * D, True))]
            t_xk1 = sch.emit("vector", lambda e, d=dbuf, xk=xkF, bc=muk_bc: e.tensor_tensor(
                feat(xk), feat(d), bc, ALU.mult),
                deps=[tok_d] + xkdep, est=v_tt(SUB * D, True))
            tok_xk = [sch.emit("vector", lambda e, xk=xkF: e.tensor_tensor(
                feat(xk), feat(xk), fF[:, :, 0:CH], ALU.add),
                deps=[t_xk1], est=v_tt(SUB * D, True))]
            p16.give(dbuf, [t_xr1, t_xk1])
            glob["tok_tailF"] = sch.emit(
                "gpsimd", lambda e: e.tensor_copy(fF[:, :, 0:1], fF[:, :, CH:CH + 1]),
                deps=[tok_d] + tok_xr + tok_xk + [glob["tok_tailF"]], est=400)

            rf32, rfdep = (p16.take() if RF16 else p32.take())
            tok_rf = []
            all_wr = []
            for s in range(SUB):
                for half in range(2):
                    ids, bdeps = banksB.take(1)
                    b = ids[0]
                    tmm = None
                    for k in range(KC):
                        prev = [tmm] if tmm is not None else []
                        tmm = sch.emit(
                            "tensor",
                            lambda e, b=b, k=k, s=s, half=half, xr=xrF: e.matmul(
                                bank(b), feat(xr)[:, k, s * 128:(s + 1) * 128],
                                wsb["fr"][:, k, half * 512:(half + 1) * 512],
                                start=(k == 0), stop=(k == KC - 1)),
                            deps=tok_xr + wcast["fr"] + bdeps + prev, est=mm_est(512))
                    all_wr.append(tmm)
                    te = sch.emit(
                        "scalar",
                        lambda e, b=b, s=s, half=half, rf=rf32: e.activation(
                            tokv(rf)[:, s, half * 512:(half + 1) * 512], bank(b),
                            AF.Sigmoid, bias=0.0, scale=1.0),
                        deps=[tmm] + (rfdep if s == 0 and half == 0 else []),
                        est=s_act(512, True), afn="sigmoid")
                    tok_rf.append(te)
                    banksB.give(ids, [te])
            p16.give(xrF, all_wr)

            krelu, krdep = p16.take()
            tok_kr = []
            all_wk = []
            for j in range(KC // 2):
                ids, bdeps = banksB.take(1)
                b = ids[0]
                tmm = None
                pair_last = []
                for half in range(2):
                    m = 2 * j + half
                    for k in range(KC):
                        prev = [tmm] if tmm is not None else []
                        tmm = sch.emit(
                            "tensor",
                            lambda e, b=b, k=k, m=m, half=half, xk=xkF: e.matmul(
                                PS[:, b * 512 + half * CH: b * 512 + half * CH + CH],
                                wsb["fk"][:, k, m * 128:(m + 1) * 128],
                                feat(xk)[:, k, :],
                                start=(k == 0), stop=(k == KC - 1)),
                            deps=tok_xk + wcast["fk"] + bdeps + prev, est=mm_est(CH))
                    all_wk.append(tmm)
                    pair_last.append(tmm)
                te = sch.emit(
                    "scalar",
                    lambda e, b=b, j=j, kr=krelu: e.activation(
                        feat(kr)[:, 2 * j:2 * j + 2, :], bank(b, 2 * CH), AF.Relu,
                        bias=0.0, scale=1.0),
                    deps=pair_last + (krdep if j == 0 else []), est=s_act(2 * CH, True))
                tok_kr.append(te)
                banksB.give(ids, [te])
            p16.give(xkF, all_wk)
            kaF, kadep = p16.take()
            tok_ka = sch.emit("vector", lambda e, ka=kaF, kr=krelu: e.tensor_tensor(
                flat(ka), flat(kr), flat(kr), ALU.mult),
                deps=tok_kr + kadep, est=v_tt(SUB * D, True))
            p16.give(krelu, [tok_ka])

            lock4 = glob["stats_lock"].pop((ps, "d"), [])
            S_ = stats[:, ps, :]
            vout = stt["vout"]
            vscr = glob.pop("vout_scr%d" % (c % N_VOUT), [])
            tok_vo = []
            all_wv = []
            for s in range(SUB):
                for half in range(2):
                    ids, bdeps = banksB.take(1)
                    b = ids[0]
                    tmm = None
                    for k in range(KC):
                        prev = [tmm] if tmm is not None else []
                        tmm = sch.emit(
                            "tensor",
                            lambda e, b=b, k=k, s=s, half=half, ka=kaF: e.matmul(
                                bank(b), feat(ka)[:, k, s * 128:(s + 1) * 128],
                                wsb["fv"][:, k, half * 512:(half + 1) * 512],
                                start=(k == 0), stop=(k == KC - 1)),
                            deps=[tok_ka] + wcast["fv"] + bdeps + prev, est=mm_est(512))
                    all_wv.append(tmm)
                    te = sch.emit(
                        "vector",
                        lambda e, b=b, s=s, half=half, rf=rf32, S_=S_, vout=vout: e.scalar_tensor_tensor(
                            tokv(vout)[:, s, half * 512:(half + 1) * 512], bank(b),
                            1.0, tokv(rf)[:, s, half * 512:(half + 1) * 512],
                            ALU.mult, ALU.mult,
                            accum_out=S_[:, 64 + s * 2 + half:65 + s * 2 + half]),
                        deps=[tmm, tok_rf[s * 2 + half], ("DO", 16 * glob["nstores"])]
                             + glob["vout_lock"][c % N_VOUT] + vscr + lock4,
                        est=(120 + 512) / 0.96)
                    tok_vo.append(te)
                    banksB.give(ids, [te])
            p16.give(kaF, all_wv)
            (p16 if RF16 else p32).give(rf32, tok_vo)

            acc4 = stats[:, ps, 64:64 + 2 * SUB].rearrange("p (a b) -> p a b", b=2)
            t1d = sch.emit("vector", lambda e, S_=S_, acc4=acc4: e.tensor_tensor(
                S_[:, 48:48 + SUB], acc4[:, :, 0], acc4[:, :, 1], ALU.add),
                deps=tok_vo, est=150)
            trs4, S_, rd4 = ln_stats(tokv(vout), ps, 48, tok_vo, "d", s1_tok=[t1d], lock=lock4, bgrp=banksB,
                                     sub_deps=[tok_vo[0:2], tok_vo[2:4]])
            tok_n4 = []
            for s in range(SUB):
                tok_n4.append(sch.emit(
                    "vector",
                    lambda e, s=s, S_=S_, vout=vout: e.tensor_scalar(
                        tokv(vout)[:, s, :], tokv(vout)[:, s, :],
                        S_[:, 52 + s:53 + s], S_[:, 58 + s:59 + s],
                        ALU.subtract, ALU.mult),
                    deps=[trs4] + rd4, est=v_ts(D)))
            glob["stats_lock"][(ps, "d")] = tok_n4 + [t1d]
            tok_xf = sch.emit("gpsimd" if XFG else "vector",
                lambda e, xb=xb, vout=vout: e.tensor_tensor(
                    flat(xb), flat(xb), flat(vout), ALU.add),
                deps=tok_n4 + stt["x2_readers"], est=3800 if XFG else v_tt(SUB * D))
            glob["vout_lock"][c % N_VOUT] = [tok_xf]
            glob["nstores"] += 1
            sch.emit("sync", dma(x_chunk_ap(c, y_h), tokv(xb)),
                     deps=[tok_xf], sem="DO", inc=16, est=5000)
            if c + NXIN < NCH:
                p2 = (c + NXIN) % NXIN
                glob["nloads"][p2] += 1
                sch.emit("sync", dma(tokv(xin[p2]), x_chunk_ap(c + NXIN, x_h)),
                         deps=[("DO", 16 * glob["nstores"])], sem="DX%d" % p2, inc=16,
                         est=5000)

        states = {}
        for r in range(NCH + 1):
            sF = emit_F_head(r) if r < NCH else None
            if r >= 1:
                emit_B1(r - 1, states[r - 1])
                emit_B2a(r - 1, states[r - 1])
            if r < NCH:
                emit_F_rest(r, sF)
                emit_F_tail(r, sF)
                states[r] = sF
            if r >= 1:
                emit_B2b(r - 1, states.pop(r - 1))

        sch.emit("sync", None, deps=[("DO", 16 * glob["nstores"])], est=100)

        streams = sch.schedule()
        t_needed = set()
        for op in sch.all:
            for d in op.deps:
                if isinstance(d, Op) and d.eng == "tensor" and op.eng != "tensor":
                    t_needed.add(d.idx)
        cnt = {k: 0 for k in ("T", "V", "S", "G")}
        dma_cnt = {}
        for e in ENGS:
            for op in streams[e]:
                if op.sem is not None:
                    dma_cnt[op.sem] = dma_cnt.get(op.sem, 0) + op.inc
                    op.inc_val = (op.sem, dma_cnt[op.sem])
                elif e in SELF_CHAIN:
                    own = ENGSEM[e]
                    cnt[own] += 1
                    op.inc_val = (own, cnt[own])
                elif e == "tensor" and op.idx in t_needed:
                    cnt["T"] += 1
                    op.inc_val = ("T", cnt["T"])

        with nc.Block() as block:
            def mk(engname):
                chain = engname in SELF_CHAIN

                def run(eng):
                    wcache = {}
                    for op in streams[engname]:
                        waits = []
                        for d in op.deps:
                            if isinstance(d, Op):
                                if d.eng == engname:
                                    continue
                                iv = d.inc_val
                                if iv is None:
                                    raise RuntimeError("dep without inc %d<-%d" % (op.idx, d.idx))
                                waits.append(iv)
                            else:
                                waits.append(d)
                        if chain and op.sem is None and op.inc_val is not None:
                            s, v = op.inc_val
                            waits.append((s, v - 1))
                        for s, v in waits:
                            if v > wcache.get(s, -1):
                                eng.wait_ge(sems[s], v)
                                wcache[s] = v
                        if op.fn is None:
                            continue
                        ins = op.fn(eng)
                        if op.inc_val is not None:
                            s, v = op.inc_val
                            ins.then_inc(sems[s], op.inc if op.sem else 1)
                return run

            block.tensor(mk("tensor"))
            block.vector(mk("vector"))
            block.scalar(mk("scalar"))
            block.gpsimd(mk("gpsimd"))
            block.sync(mk("sync"))

    return nc


_NC_CACHE = {}


def kernel(L0_attn_x, L0_attn_conv_weight, L0_attn_conv_bias, L0_attn_Wr_w,
           L0_attn_Wr_b, L0_attn_Wi_w, L0_attn_Wi_b, L0_attn_log_a,
           L0_attn_out_w, L0_attn_norm_w, L0_ffn_W_r, L0_ffn_W_k, L0_ffn_W_v,
           L0_ffn_mu_r, L0_ffn_mu_k):
    if "nc" not in _NC_CACHE:
        _NC_CACHE["nc"] = build_nc()
    nc = _NC_CACHE["nc"]

    x = np.asarray(L0_attn_x, dtype=np.float32)
    shared = {
        "wr": np.ascontiguousarray(np.asarray(L0_attn_Wr_w, dtype=np.float32).T),
        "wi": np.ascontiguousarray(np.asarray(L0_attn_Wi_w, dtype=np.float32).T),
        "ow": np.ascontiguousarray(np.asarray(L0_attn_out_w, dtype=np.float32).T),
        "fr": np.ascontiguousarray(np.asarray(L0_ffn_W_r, dtype=np.float32).T),
        "fk": np.ascontiguousarray(np.asarray(L0_ffn_W_k, dtype=np.float32).T),
        "fv": np.ascontiguousarray(np.asarray(L0_ffn_W_v, dtype=np.float32).T),
        "convw": np.ascontiguousarray(np.asarray(L0_attn_conv_weight, dtype=np.float32)[:, 0, :]),
        "convb": np.ascontiguousarray(np.asarray(L0_attn_conv_bias, dtype=np.float32)),
        "wrb": np.ascontiguousarray(np.asarray(L0_attn_Wr_b, dtype=np.float32)),
        "wib": np.ascontiguousarray(np.asarray(L0_attn_Wi_b, dtype=np.float32)),
        "loga": np.ascontiguousarray(np.asarray(L0_attn_log_a, dtype=np.float32)),
        "normw": np.ascontiguousarray(np.asarray(L0_attn_norm_w, dtype=np.float32)),
        "mur": np.ascontiguousarray(np.asarray(L0_ffn_mu_r, dtype=np.float32)),
        "muk": np.ascontiguousarray(np.asarray(L0_ffn_mu_k, dtype=np.float32)),
    }
    in_maps = []
    for b in range(8):
        m = dict(shared)
        m["x"] = np.ascontiguousarray(x[b])
        in_maps.append(m)
    res = run_bass_kernel_spmd(nc, in_maps, core_ids=list(range(8)))
    out = np.stack([np.asarray(r["y"]) for r in res.results], axis=0)
    return out.astype(np.float32)


# version 38
# speedup vs baseline: 1.0019x; 1.0019x over previous
import heapq
import numpy as np
from contextlib import ExitStack

import concourse.bass as bass
import concourse.mybir as mybir
from concourse.bass_utils import run_bass_kernel_spmd

F32 = mybir.dt.float32
BF16 = mybir.dt.bfloat16
AF = mybir.ActivationFunctionType
ALU = mybir.AluOpType
AX = mybir.AxisListType

import os
D = 1024
TOK = 4096
CH = int(os.environ.get("K_CH", "256"))
NCH = TOK // CH
KC = D // 128
SUB = CH // 128
NXIN = int(os.environ.get("K_NXIN", "2"))
NST = int(os.environ.get("K_NST", "4"))
N_P32 = int(os.environ.get("K_P32", "4"))
N_P16 = int(os.environ.get("K_P16", "8"))
N_VOUT = int(os.environ.get("K_VOUT", "1"))
BANK_MODE = os.environ.get("K_BANKS", "global")
PT_ROLE = os.environ.get("K_PTROLE", "0") == "1"
PT1 = os.environ.get("K_PT1", "0") == "1"
RF16 = os.environ.get("K_RF16", "1") == "1"
MSCAN = os.environ.get("K_MSCAN", "0") == "1"
SCAN_G = int(os.environ.get("K_SCANG", "2"))
AS16 = os.environ.get("K_AS16", "0") == "1"
HOPX = float(os.environ.get("K_HOP", "150"))
SQ2 = os.environ.get("K_SQ", "0") == "1"
XFG = os.environ.get("K_XFG", "0") == "1"
WARM_MODEL = os.environ.get("K_WARM", "1") == "1"
TBL_PICK = os.environ.get("K_TBL", "0") == "1"
LN_EPS = 1e-5
RMS_EPS = 1e-6
LOG_CLIP = float(np.log(np.float32(1e-7)))
ENGS = ("tensor", "vector", "scalar", "gpsimd", "sync")
ENGSEM = {"tensor": "T", "vector": "V", "scalar": "S", "gpsimd": "G"}
SELF_CHAIN = ("vector", "scalar", "gpsimd")
HOP_NS = 350


class Op:
    __slots__ = ("idx", "eng", "fn", "deps", "sem", "inc", "est", "inc_val", "afn")

    def __init__(self, idx, eng, fn, deps, sem, inc, est, afn):
        self.idx = idx
        self.eng = eng
        self.fn = fn
        self.deps = deps
        self.sem = sem
        self.inc = inc
        self.est = est
        self.inc_val = None
        self.afn = afn


TBL_NS = 1283.0
TBL_BONUS = float(os.environ.get("K_TBLNS", "2600"))
TBL_WIN = int(os.environ.get("K_TBLWIN", "10"))
N_JIT = int(os.environ.get("K_NJIT", "0"))
JIT_NS = float(os.environ.get("K_JITNS", "900"))
COLD_IDLE = float(os.environ.get("K_CIDLE", "1500"))
COLD_WIN = 3400.0


class Sched:
    def __init__(self):
        self.all = []
        self.base = {}
        self.dma_val = {}
        self.dma_map = {}

    def emit(self, eng, fn, deps=(), sem=None, inc=0, est=600, afn=None):
        deps = [d for d in deps if d is not None] + self.base.get(eng, [])
        if eng in SELF_CHAIN:
            est += 150.0
        op = Op(len(self.all), eng, fn, list(deps), sem, inc, est, afn)
        self.all.append(op)
        if sem is not None:
            self.dma_val[sem] = self.dma_val.get(sem, 0) + inc
            self.dma_map[(sem, self.dma_val[sem])] = op
        return op

    def schedule(self):
        global HOP_NS
        HOP_NS = HOPX
        n = len(self.all)
        by_sem = {}
        for (s, v), prod in self.dma_map.items():
            by_sem.setdefault(s, []).append((v, prod))
        for s in by_sem:
            by_sem[s].sort(key=lambda t: t[0])
        for op in self.all:
            extra = []
            for d in op.deps:
                if not isinstance(d, Op):
                    s, v = tuple(d)
                    for cv, prod in by_sem.get(s, ()):
                        if cv <= v:
                            extra.append(prod)
                        else:
                            break
            op.deps += extra
        succ = [[] for _ in range(n)]
        npred = [0] * n
        for op in self.all:
            for d in op.deps:
                if isinstance(d, Op):
                    succ[d.idx].append(op.idx)
                    npred[op.idx] += 1
        prio0 = [0.0] * n
        for i in range(n - 1, -1, -1):
            p = 0.0
            for s in succ[i]:
                if prio0[s] > p:
                    p = prio0[s]
            prio0[i] = p + self.all[i].est
        best_res = None
        rng = np.random.RandomState(12345)
        for trial in range(N_JIT + 1):
            if trial == 0:
                prio = prio0
            else:
                prio = [p + float(j) for p, j in
                        zip(prio0, rng.uniform(0.0, JIT_NS, size=n))]
            res = self._sim(n, succ, npred, prio)
            if best_res is None or res[0] < best_res[0]:
                best_res = res
        makespan, streams, finish = best_res
        self.finish = finish
        return streams

    def _sim(self, n, succ, npred, prio):
        eng_free = {e: 0.0 for e in ENGS}
        finish = [0.0] * n
        ready = {e: [] for e in ENGS}
        pend = list(npred)

        def push(i):
            op = self.all[i]
            key = (op.idx, i) if op.eng == "sync" else (-prio[i], i)
            heapq.heappush(ready[op.eng], key)

        for i in range(n):
            if pend[i] == 0:
                push(i)
        streams = {e: [] for e in ENGS}
        remaining = n
        cur_tbl = None
        pe_last_end = -1e18
        pe_cold_until = -1e18

        def pick_scalar():
            tmp = []
            while ready["scalar"] and len(tmp) < TBL_WIN:
                tmp.append(heapq.heappop(ready["scalar"]))
            best_key, best_adj = None, None
            for key in tmp:
                op = self.all[key[-1]]
                adj = -key[0] + (TBL_BONUS if (op.afn and op.afn == cur_tbl) else 0.0)
                if best_adj is None or adj > best_adj:
                    best_adj, best_key = adj, key
            for key in tmp:
                if key is not best_key:
                    heapq.heappush(ready["scalar"], key)
            return best_key

        while remaining:
            best = None
            scalar_key = None
            for e in ENGS:
                if not ready[e]:
                    continue
                if e == "scalar" and TBL_PICK:
                    scalar_key = pick_scalar()
                    i = scalar_key[-1]
                else:
                    i = ready[e][0][-1]
                op = self.all[i]
                dep_t = 0.0
                for d in op.deps:
                    if isinstance(d, Op):
                        t = finish[d.idx]
                        if d.eng != op.eng:
                            t += HOP_NS
                        if t > dep_t:
                            dep_t = t
                start = max(eng_free[e], dep_t)
                if best is None or start < best[0]:
                    best = (start, e, i)
            start, e, i = best
            if e != "scalar":
                if scalar_key is not None:
                    heapq.heappush(ready["scalar"], scalar_key)
                heapq.heappop(ready[e])
            elif scalar_key is None:
                heapq.heappop(ready[e])
            op = self.all[i]
            dur = op.est
            if e == "scalar" and op.afn:
                if op.afn != cur_tbl:
                    dur += TBL_NS
                    cur_tbl = op.afn
            if e == "tensor" and WARM_MODEL:
                if start - pe_last_end > COLD_IDLE:
                    pe_cold_until = start + COLD_WIN
                if start < pe_cold_until:
                    dur = dur * 2.0
                pe_last_end = start + dur
            finish[i] = start + dur
            eng_free[e] = finish[i]
            streams[e].append(op)
            remaining -= 1
            for s in succ[i]:
                pend[s] -= 1
                if pend[s] == 0:
                    push(s)
        return max(finish), streams, finish


class Banks:
    def __init__(self, base, nb):
        self.base = base
        self.nb = nb
        self.free = {}
        self.ptr = 0

    def take(self, n):
        if self.ptr + n > self.nb:
            self.ptr = 0
        ids = [self.base + self.ptr + i for i in range(n)]
        self.ptr = (self.ptr + n) % self.nb
        deps = [t for i in ids for t in self.free.get(i, [])]
        return ids, deps

    def give(self, ids, toks):
        for i in ids:
            self.free[i] = list(toks)


class Pool:
    def __init__(self, slots):
        self.q = [(h, []) for h in slots]

    def take(self):
        best = min(range(len(self.q)),
                   key=lambda i: max([t.idx for t in self.q[i][1]
                                      if isinstance(t, Op)], default=-1))
        return self.q.pop(best)

    def give(self, h, toks):
        self.q.append((h, list(toks)))


def v_tt(f, bf=False):
    return (151 + (f // 2 if bf else f)) / 0.96


def v_ts(f, bf=False):
    return (58 + (f // 4 if bf else f // 2)) / 0.96


def v_red(f):
    return (58 + f) / 0.96


def s_act(f, psum=False):
    return ((330 if psum else 280) + f) / 1.2


def mm_est(nn):
    return nn / 2.4 + (12 if WARM_MODEL else 45)


def build_nc():
    nc = bass.Bass("TRN2", target_bir_lowering=False)

    x_h = nc.dram_tensor("x", [TOK, D], F32, kind="ExternalInput")
    y_h = nc.dram_tensor("y", [TOK, D], F32, kind="ExternalOutput")
    wnames = ("wr", "wi", "ow", "fr", "fk", "fv")
    w_h = {n: nc.dram_tensor(n, [D, D], F32, kind="ExternalInput") for n in wnames}
    convw_h = nc.dram_tensor("convw", [D, 4], F32, kind="ExternalInput")
    vecnames = ("convb", "wrb", "wib", "loga", "mur", "muk")
    v_h = {n: nc.dram_tensor(n, [D], F32, kind="ExternalInput") for n in vecnames}
    normw_h = nc.dram_tensor("normw", [D], F32, kind="ExternalInput")

    sch = Sched()
    if BANK_MODE == "global":
        banksF = banksM = banksB = Banks(0, 7 if PT1 else 6)
    else:
        banksF = Banks(0, 2)
        banksM = Banks(2, 2)
        banksB = Banks(4, 2)

    with ExitStack() as st:
        def sb(name, shape, dt):
            return st.enter_context(nc.sbuf_tensor(name, shape, dt))

        sems = {}
        for sname in (("T", "V", "S", "G", "DC", "DO", "DB") + tuple("DW%d" % i for i in range(6))
                      + tuple("DX%d" % i for i in range(NXIN))):
            sems[sname] = st.enter_context(nc.semaphore(sname))

        _wshrink = os.environ.get("K_WSHRINK", "0") == "1"
        if _wshrink:
            _wshared = sb("w_shared", [128, KC, D], BF16)
            wsb = {n: _wshared for n in wnames}
        else:
            wsb = {n: sb("w_" + n, [128, KC, D], BF16) for n in wnames}
        ident = sb("ident", [128, 128], BF16)
        diag = sb("diag", [128, 32, 128], BF16)
        ones16 = sb("ones16", [128, CH], BF16)
        wconv = sb("wconv", [128, KC, 4], F32)
        cols = {n: sb("c_" + n, [128, KC], F32) for n in vecnames}
        gbias = sb("gbias", [65, D], BF16)
        lnsig = cols["loga"]
        constvals = [1e-5, 1e-6, 1.0, 2.0, -1.0, 1.0 / D]
        cdb = sb("cdb", [128, len(constvals)], F32)
        for i, v in enumerate(constvals):
            nc.const_aps.register(F32, float(v), cdb[:, i:i + 1])
        normw_rep = sb("normw_rep", [128, D], BF16)
        zeros32 = sb("zeros32", [128, CH], F32)
        ln1F = sb("ln1F", [128, KC, CH + 3], BF16)
        fF = sb("fF", [128, KC, CH + 1], BF16)
        scanmask = sb("scanmask", [128, SUB * D], BF16) if MSCAN else None
        carryC = sb("carryC", [128, KC], F32)
        carryS = sb("carryS", [128, KC], F32)
        ctmp = sb("ctmp", [128, KC], F32)
        stmp = sb("stmp", [128, KC], F32)
        stats = sb("stats", [128, NST, 96], F32)
        xin = [sb("xin%d" % i, [128, SUB * D], F32) for i in range(NXIN)]
        vouts = [sb("vout%d" % i, [128, SUB * D], F32) for i in range(N_VOUT)]
        p32 = Pool([sb("p32_%d" % i, [128, SUB * D], F32) for i in range(N_P32)])
        p16 = Pool([sb("p16_%d" % i, [128, SUB * D], BF16) for i in range(N_P16)])
        PS = st.enter_context(nc.psum_tensor("PS", [128, 3584 if PT1 else 3072], F32))
        PT = st.enter_context(nc.psum_tensor("PT", [128, 1024 if PT1 else 2048], BF16))
        tb_free = [[], []]

        def bank(b, n=512):
            return PS[:, b * 512: b * 512 + n]

        def bank2(b):
            return PS[:, b * 512: b * 512 + 1024]

        def tbank(t):
            return PT[:, t * 1024:(t + 1) * 1024]

        def feat(h, c=KC):
            return h[:, :].rearrange("p (c t) -> p c t", c=c)

        def tokv(h):
            return h[:, :].rearrange("p (s d) -> p s d", s=SUB)

        def flat(h):
            return h[:, :]

        def dma(dst, src):
            return lambda eng, o=dst, i=src: eng.dma_start(o, i)

        def dma_nc(dst, src):
            return lambda eng, o=dst, i=src: eng.dma_start(o, i, allow_slow_non_contiguous=True)

        ndc = 0
        for name in ("loga", "mur", "muk", "wrb", "wib"):
            sch.emit("sync", dma_nc(cols[name][:, :], v_h[name][:].rearrange("(k p) -> p k", p=128)),
                     sem="DC", inc=16, est=3000)
            ndc += 1
        sch.emit("sync", dma(wconv[:, :, :], convw_h[:, :].rearrange("(k p) t -> p k t", p=128)),
                 sem="DC", inc=16, est=3000)
        ndc += 1
        sch.emit("gpsimd", dma(normw_rep[0:1, :], normw_h[:].rearrange("(a d) -> a d", a=1)),
                 sem="DB", inc=16, est=3000)
        tok_dc = ("DC", 16 * ndc)
        tok_db = ("DB", 32)
        tok_dcg = tok_dc
        sch.emit("gpsimd", dma(gbias[64:65, :], v_h["convb"][:].rearrange("(a d) -> a d", a=1)),
                 sem="DB", inc=16, est=2000)

        wcast = {}
        for wi_, n in enumerate(wnames):
            wcast[n] = [sch.emit(
                "gpsimd",
                dma(wsb[n][:, :, :], w_h[n][:, :].rearrange("(k p) d -> p k d", p=128)),
                sem="DW%d" % wi_, inc=16, est=12000)]

        def x_chunk_ap(c, h):
            return h[c * CH:(c + 1) * CH, :].rearrange("(s p) d -> p s d", p=128)

        tok_cdb = None
        for i, v in enumerate(constvals):
            tok_cdb = sch.emit("gpsimd", lambda e, i=i, v=v: e.memset(cdb[:, i:i + 1], float(v)),
                               deps=[tok_cdb], est=300)
        sch.base["scalar"] = [tok_cdb]
        tok_i1 = sch.emit("gpsimd", lambda e: e.iota(
            zeros32[:, 0:128], [[1, 128]], channel_multiplier=-1, allow_small_or_imprecise_dtypes=True),
            est=400)
        tok_id = sch.emit("vector", lambda e: e.tensor_single_scalar(ident[:, :], zeros32[:, 0:128], 0.0, ALU.is_equal),
                          deps=[tok_i1], est=300)
        tok_diag = []
        for ch in range(KC):
            for kk in range(4):
                scl = wconv[:, ch, kk:kk + 1]
                tok_diag.append(sch.emit(
                    "scalar",
                    lambda e, i=ch * 4 + kk, scl=scl: e.activation(
                        diag[:, i, :], ident[:, :], AF.Copy,
                        bias=0.0, scale=scl),
                    deps=[tok_id, tok_dc, tok_cdb], est=300))
        tok_ones16 = sch.emit("gpsimd", lambda e: e.memset(ones16[:, :], 1.0), est=300)
        t_sig = sch.emit("scalar", lambda e: e.activation(lnsig[:, :], cols["loga"][:, :], AF.Sigmoid, bias=0.0, scale=1.0),
                         deps=[tok_dc], est=300)
        tok_ln0 = sch.emit("scalar", lambda e: e.activation(lnsig[:, :], lnsig[:, :], AF.Ln, bias=0.0, scale=1.0),
                           deps=[t_sig], est=300)
        tok_lnsig = sch.emit("vector", lambda e: e.tensor_single_scalar(lnsig[:, :], lnsig[:, :], 8.0, ALU.mult),
                             deps=[tok_ln0], est=200)
        tok_nw = []
        for half in range(2):
            ids, bdeps = banksM.take(1)
            b = ids[0]
            tmb = sch.emit(
                "tensor",
                lambda e, b=b, half=half: e.matmul(
                    bank(b), ones16[0:1, 0:128],
                    normw_rep[0:1, half * 512:(half + 1) * 512],
                    start=True, stop=True),
                deps=[tok_ones16, tok_db] + bdeps, est=2000)
            ev = sch.emit(
                "scalar",
                lambda e, b=b, half=half: e.activation(
                    normw_rep[:, half * 512:(half + 1) * 512], bank(b), AF.Copy,
                    bias=0.0, scale=1.0),
                deps=[tmb], est=s_act(512, True))
            tok_nw.append(ev)
            banksM.give(ids, [ev])
        tok_z32 = sch.emit("gpsimd", lambda e: e.memset(zeros32[:, :], 0.0),
                           deps=[tok_id], est=400)
        if MSCAN:
            tok_mask = sch.emit("gpsimd", lambda e: e.memset(scanmask[:, :], 1.0), est=400)
            tok_mask = sch.emit("gpsimd", lambda e: e.memset(feat(scanmask)[:, :, 0], 0.0),
                                deps=[tok_mask], est=300)
        else:
            tok_mask = None
        tok_halo1 = sch.emit("gpsimd", lambda e: e.memset(ln1F[:, :, 0:3], 0.0), est=300)
        tok_cc0 = sch.emit("gpsimd", lambda e: e.memset(carryC[:, :], 0.0), est=200)
        tok_cs0 = sch.emit("gpsimd", lambda e: e.memset(carryS[:, :], 0.0), est=200)
        glob_carry = {"C": tok_cc0, "S": tok_cs0}
        tok_haloF = sch.emit("gpsimd", lambda e: e.memset(fF[:, :, 0:1], 0.0), est=300)

        glob = {
            "nloads": [0] * NXIN, "nstores": 0,
            "xin_free": [[] for _ in range(NXIN)],
            "tok_tail1": tok_halo1, "tok_tailF": tok_haloF,
            "stats_lock": {},
            "vout_lock": [[] for _ in range(N_VOUT)],
        }

        for c in range(min(NXIN, NCH)):
            par = c % NXIN
            glob["nloads"][par] += 1
            sch.emit("sync", dma(tokv(xin[par]), x_chunk_ap(c, x_h)),
                     deps=glob["xin_free"][par], sem="DX%d" % par, inc=16,
                     est=5000)

        def ln_stats(xap, par, base, src_deps, tag, s1_tok=None, lock=None, bgrp=None,
                     sub_deps=None):
            banks = bgrp
            S_ = stats[:, par, :]
            if lock is None:
                lock = glob["stats_lock"].pop((par, tag), [])
            if s1_tok is None:
                t1 = [sch.emit("vector", lambda e, S_=S_, xap=xap, base=base: e.tensor_reduce(
                    S_[:, base:base + SUB], xap, AX.X, ALU.add),
                    deps=src_deps + lock, est=v_red(SUB * D))]
            else:
                t1 = list(s1_tok)
            ids, bdeps = banks.take(2)
            b = ids[0]
            tsq = []
            for s in range(SUB):
                sd = sub_deps[s] if sub_deps is not None else src_deps
                tsq.append(sch.emit(
                    "scalar",
                    lambda e, s2=s, b=b, S_=S_, xap=xap, base=base: e.activation(
                        bank2(b), xap[:, s2, :], AF.Square, bias=0.0, scale=1.0,
                        accum_out=S_[:, base + 2 + s2:base + 3 + s2]),
                    deps=sd + bdeps + lock, est=s_act(D)))
            banks.give(ids, tsq)
            tm_ = sch.emit("vector", lambda e, S_=S_, base=base: e.tensor_single_scalar(
                S_[:, base + 4:base + 4 + SUB], S_[:, base:base + SUB], 1.0 / D, ALU.mult),
                deps=t1, est=150)
            tmm = sch.emit("vector", lambda e, S_=S_, base=base: e.tensor_tensor(
                S_[:, base + 8:base + 8 + SUB], S_[:, base + 4:base + 4 + SUB], S_[:, base + 4:base + 4 + SUB], ALU.mult),
                deps=[tm_], est=150)
            tq = sch.emit("vector", lambda e, S_=S_, base=base: e.scalar_tensor_tensor(
                S_[:, base + 8:base + 8 + SUB], S_[:, base + 2:base + 2 + SUB], 1.0 / D,
                S_[:, base + 8:base + 8 + SUB], ALU.mult, ALU.subtract),
                deps=tsq + [tmm], est=150)
            tln = sch.emit("scalar", lambda e, S_=S_, base=base: e.activation(
                S_[:, base + 14:base + 14 + SUB], S_[:, base + 8:base + 8 + SUB], AF.Ln, bias=LN_EPS, scale=1.0),
                deps=[tq], est=250, afn="exp")
            trs = sch.emit("scalar", lambda e, S_=S_, base=base: e.activation(
                S_[:, base + 10:base + 10 + SUB], S_[:, base + 14:base + 14 + SUB], AF.Exp, bias=0.0, scale=-0.5),
                deps=[tln], est=250, afn="exp")
            return trs, S_, t1 + tsq

        def emit_F_head(c):
            par = c % NXIN
            ps = c % NST
            xb = xin[par]
            xt = tokv(xb)
            dxs = "DX%d" % par
            tok_x = (dxs, 16 * glob["nloads"][par])
            stt = {"par": par, "ps": ps, "xb": xb, "xt": xt, "tok_x": tok_x}

            trs1, S_, rd1 = ln_stats(xt, ps, 0, [tok_x], "a", bgrp=banksF)
            ln1, ln1dep = p16.take()
            tok_ln1 = []
            for s in range(SUB):
                tok_ln1.append(sch.emit(
                    "vector",
                    lambda e, s2=s, S_=S_, ln1=ln1, xt=xt: e.tensor_scalar(
                        tokv(ln1)[:, s2, :], xt[:, s2, :],
                        S_[:, 4 + s2:5 + s2], S_[:, 10 + s2:11 + s2],
                        ALU.subtract, ALU.mult),
                    deps=[trs1, tok_x] + ln1dep, est=v_ts(D)))
            glob["stats_lock"][(ps, "a")] = tok_ln1
            stt["xb_readers"] = rd1 + tok_ln1

            tok_tr1 = []
            tok_ev1 = []
            for s2 in range(SUB):
                tb = 0 if (PT1 or PT_ROLE) else s2
                trs_ = []
                for j in range(KC):
                    trs_.append(sch.emit(
                        "tensor",
                        lambda e, tb=tb, j=j, s2=s2, ln1=ln1: e.transpose(
                            tbank(tb)[:, j * 128:(j + 1) * 128],
                            tokv(ln1)[:, s2, j * 128:(j + 1) * 128],
                            ident[:, :]),
                        deps=[tok_ln1[s2], tok_id] + tb_free[tb], est=65))
                tev = sch.emit(
                    "scalar",
                    lambda e, tb=tb, s2=s2: e.activation(
                        ln1F[:, 0:KC, 3 + s2 * 128: 3 + (s2 + 1) * 128],
                        tbank(tb), AF.Copy, bias=0.0, scale=1.0),
                    deps=trs_ + [glob["tok_tail1"]], est=s_act(D, True))
                tok_ev1.append(tev)
                tok_tr1 += trs_
                tb_free[tb] = [tev]
            p16.give(ln1, tok_tr1)
            stt["tok_ev1"] = tok_ev1
            return stt

        def emit_F_rest(c, stt):
            tok_ev1 = stt["tok_ev1"]
            xcFh, xcFdep = p16.take()
            tok_xcF = []
            tmm_pair = []
            for pr in range(KC // 2):
                ids, bdeps = banksF.take(1)
                b = ids[0]
                tm = None
                for half in range(2):
                    ch = pr * 2 + half
                    for kk in range(4):
                        prev = [tm] if tm is not None else []
                        tm = sch.emit(
                            "tensor",
                            lambda e, b=b, ch=ch, kk=kk, half=half: e.matmul(
                                PS[:, b * 512 + half * CH: b * 512 + half * CH + CH],
                                diag[:, ch * 4 + kk, :], ln1F[:, ch, kk:kk + CH],
                                start=(kk == 0), stop=False),
                            deps=tok_ev1 + [tok_diag[ch * 4 + kk]] + bdeps + prev
                                 + [glob["tok_tail1"]],
                            est=mm_est(CH))
                    tm = sch.emit(
                        "tensor",
                        lambda e, b=b, ch=ch, half=half: e.matmul(
                            PS[:, b * 512 + half * CH: b * 512 + half * CH + CH],
                            gbias[64:65, ch * 128:(ch + 1) * 128], ones16[64:65, :],
                            start=False, stop=True),
                        deps=[tm, tok_db, tok_ones16], est=mm_est(CH))
                tmm_pair.append(tm)
                t2 = sch.emit(
                    "scalar",
                    lambda e, b=b, pr=pr: e.activation(
                        feat(xcFh)[:, pr * 2:pr * 2 + 2, :], bank(b, 2 * CH), AF.Copy,
                        bias=0.0, scale=1.0),
                    deps=[tm] + (xcFdep if pr == 0 else []), est=s_act(512, True))
                tok_xcF.append(t2)
                banksF.give([b], [t2])
            xcF = xcFh
            glob["tok_tail1"] = sch.emit(
                "gpsimd", lambda e: e.tensor_copy(ln1F[:, :, 0:3], ln1F[:, :, CH:CH + 3]),
                deps=tmm_pair + [glob["tok_tail1"]], est=500)

            r16, r16dep = p16.take()
            i16, i16dep = p16.take()
            all_gate_mm = []
            for wn, biasn, gb, gdep in (("wr", "wrb", r16, r16dep),
                                        ("wi", "wib", i16, i16dep)):
                toks = []
                for m in range(KC):
                    ids, bdeps = banksF.take(1)
                    b = ids[0]
                    tmm = None
                    for k in range(KC):
                        prev = [tmm] if tmm is not None else []
                        tmm = sch.emit(
                            "tensor",
                            lambda e, b=b, wn=wn, k=k, m=m, xcF=xcF: e.matmul(
                                bank(b, CH), wsb[wn][:, k, m * 128:(m + 1) * 128],
                                feat(xcF)[:, k, :],
                                start=(k == 0), stop=(k == KC - 1)),
                            deps=tok_xcF + wcast[wn] + bdeps + prev,
                            est=mm_est(CH))
                    all_gate_mm.append(tmm)
                    te = sch.emit(
                        "scalar",
                        lambda e, b=b, gb=gb, m=m, biasn=biasn: e.activation(
                            feat(gb)[:, m, :], bank(b, CH), AF.Sigmoid,
                            bias=cols[biasn][:, m:m + 1], scale=1.0),
                        deps=[tmm, tok_dcg] + (gdep if m == 0 else []),
                        est=s_act(CH, True), afn="sigmoid")
                    toks.append(te)
                    banksF.give(ids, [te])
                stt[wn + "_toks"] = toks
            stt["xcF"] = xcF
            stt["tok_xcF"] = tok_xcF
            stt["r16"] = r16
            stt["i16"] = i16
            stt["gate_mm"] = all_gate_mm

        NSPL = int(os.environ.get("K_NSPL", "1"))
        NHF = SUB * D // NSPL
        CPS = KC // NSPL

        def half(h, j):
            return h[:, j * NHF:(j + 1) * NHF]

        def emit_F_tail(c, stt):
            z, zdep = p32.take()
            tok_z = []
            for j in range(NSPL):
                bc = bass.AP(lnsig, CPS * j, [[KC, 128], [1, CPS], [0, CH]])
                tok_z.append(sch.emit(
                    "vector",
                    lambda e, j=j, z=z, r16=stt["r16"], bc=bc: e.tensor_tensor(
                        feat(half(z, j), CPS), feat(half(r16, j), CPS), bc, ALU.mult),
                    deps=stt["wr_toks"][CPS * j:CPS * (j + 1)] + [tok_lnsig]
                         + (zdep if j == 0 else []), est=v_tt(NHF, True)))
            p16.give(stt["r16"], tok_z)
            tbuf, tdep = p16.take()
            tok_t = []
            for j in range(NSPL):
                tok_t.append(sch.emit(
                    "vector",
                    lambda e, j=j, t=tbuf, i16=stt["i16"], xcF=stt["xcF"]: e.tensor_tensor(
                        half(t, j), half(i16, j), half(xcF, j), ALU.mult),
                    deps=stt["wi_toks"][CPS * j:CPS * (j + 1)] + stt["tok_xcF"][(CPS // 2) * j:(CPS // 2) * (j + 1)]
                         + (tdep if j == 0 else []), est=v_tt(NHF, True)))
            p16.give(stt["i16"], tok_t)
            p16.give(stt["xcF"], tok_t + stt["gate_mm"])
            stt["z"] = z
            stt["tok_z"] = tok_z
            stt["t"] = tbuf
            stt["tok_t"] = tok_t

        def emit_B1(c, stt):
            z, tbuf = stt["z"], stt["t"]

            la, ladep = p32.take()
            tok_la = []
            for j in range(NSPL):
                tok_la.append(sch.emit(
                    "vector", lambda e, j=j, la=la, z=z: e.tensor_single_scalar(
                        half(la, j), half(z, j), LOG_CLIP, ALU.max),
                    deps=[stt["tok_z"][j]] + (ladep if j == 0 else []), est=v_ts(NHF)))
            sbuf_s, sdep = p32.take()
            tok_s = []
            for j in range(NSPL):
                tok_s.append(sch.emit(
                    "scalar", lambda e, j=j, s=sbuf_s, z=z: e.activation(
                        half(s, j), half(z, j), AF.Exp, bias=0.0, scale=2.0),
                    deps=[stt["tok_z"][j]] + (sdep if j == 0 else []),
                    est=s_act(NHF), afn="exp"))
            p32.give(z, tok_s + tok_la)
            tok_sp = [sch.emit(
                "vector", lambda e, j=j, s=sbuf_s: e.tensor_single_scalar(
                    half(s, j), half(s, j), 1.0 - 1e-6, ALU.min),
                deps=[tok_s[j]], est=v_ts(NHF)) for j in range(NSPL)]

            sq16, sqdep16 = p16.take()
            tok_sq = []
            for j in range(NSPL):
                if SQ2:
                    tok_lnw = sch.emit("scalar", lambda e, j=j, s=sbuf_s: e.activation(
                        half(s, j), half(s, j), AF.Ln, bias=1.0, scale=-1.0),
                        deps=[tok_sp[j]], est=s_act(NHF), afn="exp")
                    tok_sq.append(sch.emit(
                        "scalar", lambda e, j=j, s=sbuf_s, sq=sq16: e.activation(
                            half(sq, j), half(s, j), AF.Exp, bias=0.0, scale=0.5),
                        deps=[tok_lnw] + (sqdep16 if j == 0 else []),
                        est=s_act(NHF), afn="exp"))
                else:
                    tok_sq.append(sch.emit(
                        "scalar", lambda e, j=j, s=sbuf_s, sq=sq16: e.activation(
                            half(sq, j), half(s, j), AF.Sqrt, bias=1.0, scale=-1.0),
                        deps=[tok_sp[j]] + (sqdep16 if j == 0 else []),
                        est=s_act(NHF), afn="sqrt"))
            p32.give(sbuf_s, tok_sq)
            tok_u1 = [sch.emit(
                "vector", lambda e, j=j, t=tbuf, sq=sq16: e.tensor_tensor(
                    half(t, j), half(sq, j), half(t, j), ALU.mult),
                deps=[tok_sq[j], stt["tok_t"][j]], est=v_tt(NHF, True))
                for j in range(NSPL)]
            p16.give(sq16, tok_u1)
            cbuf, cdep = p32.take()
            if MSCAN:
                W = SCAN_G * CH
                tok_c = [sch.emit(
                    "vector",
                    lambda e, j=j, cb=cbuf, la=la: e.tensor_tensor_scan(
                        flat(cb)[:, j * W:(j + 1) * W], flat(scanmask)[:, j * W:(j + 1) * W],
                        flat(la)[:, j * W:(j + 1) * W],
                        0.0, ALU.mult, ALU.add),
                    deps=[tok_mask] + tok_la + cdep, est=(400 + W) / 0.96)
                    for j in range(KC // SCAN_G)]
            else:
                tok_c = []
                for m in range(KC):
                    tok_c.append(sch.emit(
                        "vector",
                        lambda e, m=m, cb=cbuf, la=la: e.tensor_tensor_scan(
                            feat(cb)[:, m, :], feat(la)[:, m, :], zeros32[:, :],
                            0.0, ALU.add, ALU.add),
                        deps=[tok_z32] + tok_la + cdep, est=(400 + CH) / 0.96))
            p32.give(la, tok_c)
            prevC = glob_carry["C"]
            tlast = sch.emit("vector", lambda e, cb=cbuf: e.tensor_copy(
                ctmp[:, :], feat(cb)[:, :, CH - 1]), deps=tok_c, est=150)
            tok_m_l = []
            for m in range(KC):
                tok_m_l.append(sch.emit(
                    "vector",
                    lambda e, m=m, cb=cbuf: e.tensor_scalar(
                        feat(cb)[:, m, :], feat(cb)[:, m, :],
                        carryC[:, m:m + 1], -80.0, ALU.add, ALU.max),
                    deps=[tlast, prevC] + tok_c, est=v_ts(CH)))
            glob_carry["C"] = sch.emit(
                "vector", lambda e: e.tensor_tensor(
                    carryC[:, :], carryC[:, :], ctmp[:, :], ALU.add),
                deps=[tlast, prevC] + tok_m_l, est=150)
            tok_m = tok_m_l[-1]
            Ainv, Aidep = p16.take()
            tok_Ai = [sch.emit("scalar", lambda e, j=j, A=Ainv, cb=cbuf: e.activation(
                half(A, j), half(cb, j), AF.Exp, bias=0.0, scale=-1.0),
                deps=tok_m_l[CPS * j:CPS * (j + 1)] + (Aidep if j == 0 else []),
                est=s_act(NHF), afn="exp") for j in range(NSPL)]
            tok_u = [sch.emit("vector", lambda e, j=j, t=tbuf, A=Ainv: e.tensor_tensor(
                half(t, j), half(t, j), half(A, j), ALU.mult),
                deps=[tok_Ai[j], tok_u1[j]], est=v_tt(NHF, True)) for j in range(NSPL)]
            p16.give(Ainv, tok_u)
            Ssc, Sdep = (p16.take() if AS16 else p32.take())
            if MSCAN:
                W = SCAN_G * CH
                tok_S = [sch.emit(
                    "vector",
                    lambda e, j=j, S2=Ssc, t=tbuf: e.tensor_tensor_scan(
                        flat(S2)[:, j * W:(j + 1) * W], flat(scanmask)[:, j * W:(j + 1) * W],
                        flat(t)[:, j * W:(j + 1) * W],
                        0.0, ALU.mult, ALU.add),
                    deps=[tok_mask] + tok_u + Sdep, est=(400 + W) / 0.96)
                    for j in range(KC // SCAN_G)]
            else:
                tok_S = []
                for m in range(KC):
                    tok_S.append(sch.emit(
                        "vector",
                        lambda e, m=m, S2=Ssc, t=tbuf: e.tensor_tensor_scan(
                            feat(S2)[:, m, :], feat(t)[:, m, :], zeros32[:, :],
                            0.0, ALU.add, ALU.add),
                        deps=[tok_z32] + tok_u + Sdep, est=(400 + CH // 2) / 0.96))
            p16.give(tbuf, tok_S)
            prevS = glob_carry["S"]
            slast = sch.emit("vector", lambda e, S2=Ssc: e.tensor_copy(
                stmp[:, :], feat(S2)[:, :, CH - 1]), deps=tok_S, est=150)
            Abuf, Adep = (p16.take() if AS16 else p32.take())
            tok_A = [sch.emit("scalar", lambda e, j=j, A=Abuf, cb=cbuf: e.activation(
                half(A, j), half(cb, j), AF.Exp, bias=0.0, scale=1.0),
                deps=tok_m_l[CPS * j:CPS * (j + 1)] + (Adep if j == 0 else []),
                est=s_act(NHF), afn="exp") for j in range(NSPL)]
            p32.give(cbuf, tok_A + tok_Ai)
            hF, hdep = p16.take()
            tok_h_l = []
            for m in range(KC):
                tok_h_l.append(sch.emit(
                    "vector",
                    lambda e, m=m, h=hF, A=Abuf, S2=Ssc: e.scalar_tensor_tensor(
                        feat(h)[:, m, :], feat(S2)[:, m, :], carryS[:, m:m + 1],
                        feat(A)[:, m, :], ALU.add, ALU.mult),
                    deps=tok_A + [prevS] + tok_S + hdep,
                    est=(151 + (CH // 2 if AS16 else CH)) / 0.96))
            glob_carry["S"] = sch.emit(
                "vector", lambda e: e.tensor_tensor(
                    carryS[:, :], carryS[:, :], stmp[:, :], ALU.add),
                deps=[slast, prevS] + tok_h_l, est=150)
            tok_h = tok_h_l[-1]
            (p16 if AS16 else p32).give(Abuf, tok_h_l)
            (p16 if AS16 else p32).give(Ssc, tok_h_l)
            stt["hF"] = hF
            stt["tok_h"] = tok_h_l

        def emit_B2a(c, stt):
            par = stt["par"]
            ps = stt["ps"]
            xb = stt["xb"]
            xt = stt["xt"]
            hF = stt["hF"]
            tok_h_l = stt["tok_h"]

            vout = vouts[c % N_VOUT]
            stt["vout"] = vout
            o2, o2dep = p32.take()
            S_ = stats[:, ps, :]
            rmslock = glob["stats_lock"].pop((ps, "rms%d" % (c % N_VOUT)), [])
            lock2 = glob["stats_lock"].pop((ps, "b"), [])
            tok_o2 = []
            all_ow = []
            for s in range(SUB):
                ids, bdeps = banksM.take(2)
                tok_half = []
                for half in range(2):
                    b = ids[half]
                    tmm = None
                    for k in range(KC):
                        prev = [tmm] if tmm is not None else []
                        tmm = sch.emit(
                            "tensor",
                            lambda e, b=b, k=k, s=s, half=half, hF=hF: e.matmul(
                                bank(b),
                                feat(hF)[:, k, s * 128:(s + 1) * 128],
                                wsb["ow"][:, k, half * 512:(half + 1) * 512],
                                start=(k == 0), stop=(k == KC - 1)),
                            deps=tok_h_l + wcast["ow"] + bdeps + prev, est=mm_est(512))
                    tok_half.append(tmm)
                all_ow += tok_half
                tsq = []
                for half in range(2):
                    tsq.append(sch.emit(
                        "scalar",
                        lambda e, b=ids[half], s=s, half=half, S_=S_, vout=vout: e.activation(
                            tokv(vout)[:, s, half * 512:(half + 1) * 512], bank(b),
                            AF.Square, bias=0.0, scale=1.0,
                            accum_out=S_[:, 80 + s * 2 + half:81 + s * 2 + half]),
                        deps=[tok_half[half]] + rmslock + glob["vout_lock"][c % N_VOUT]
                             + ([tsq[-1]] if tsq else []),
                        est=s_act(512, True)))
                glob.setdefault("vout_scr%d" % (c % N_VOUT), [])
                glob["vout_scr%d" % (c % N_VOUT)] += tsq
                tadd = sch.emit("vector", lambda e, s=s, S_=S_: e.tensor_tensor(
                    S_[:, 84 + s:85 + s], S_[:, 80 + s * 2:81 + s * 2],
                    S_[:, 81 + s * 2:82 + s * 2], ALU.add), deps=tsq, est=150)
                tlnr = sch.emit("scalar", lambda e, s=s, S_=S_: e.activation(
                    S_[:, 88 + s:89 + s], S_[:, 84 + s:85 + s], AF.Ln,
                    bias=RMS_EPS, scale=1.0 / D), deps=[tadd], est=250, afn="exp")
                trms = sch.emit("scalar", lambda e, s=s, S_=S_: e.activation(
                    S_[:, 86 + s:87 + s], S_[:, 88 + s:89 + s], AF.Exp, bias=0.0, scale=-0.5),
                    deps=[tlnr], est=250, afn="exp")
                evs = []
                for half in range(2):
                    evs.append(sch.emit(
                        "vector",
                        lambda e, b=ids[half], s=s, half=half, o2=o2, S_=S_: e.scalar_tensor_tensor(
                            tokv(o2)[:, s, half * 512:(half + 1) * 512], bank(b),
                            S_[:, 86 + s:87 + s], normw_rep[:, half * 512:(half + 1) * 512],
                            ALU.mult, ALU.mult,
                            accum_out=S_[:, 60 + s * 2 + half:61 + s * 2 + half]),
                        deps=[trms, tok_half[half]] + tok_nw + lock2 + (o2dep if s == 0 and half == 0 else []),
                        est=(120 + 512) / 0.96))
                tok_o2 += evs
                glob["stats_lock"][(ps, "rms%d" % (c % N_VOUT))] = [tlnr, trms] + evs
                banksM.give(ids, tsq + evs)
            p16.give(hF, all_ow)

            acc2 = stats[:, ps, 60:60 + 2 * SUB].rearrange("p (a b) -> p a b", b=2)
            t1b = sch.emit("vector", lambda e, S_=S_, acc2=acc2: e.tensor_tensor(
                S_[:, 16:16 + SUB], acc2[:, :, 0], acc2[:, :, 1], ALU.add),
                deps=tok_o2, est=150)
            trs2, S_, rd2 = ln_stats(tokv(o2), ps, 16, tok_o2, "b", s1_tok=[t1b], lock=lock2, bgrp=banksM,
                                     sub_deps=[tok_o2[0:2], tok_o2[2:4]])
            tok_n2 = []
            for s in range(SUB):
                tok_n2.append(sch.emit(
                    "vector",
                    lambda e, s=s, o2=o2, S_=S_: e.tensor_scalar(
                        tokv(o2)[:, s, :], tokv(o2)[:, s, :],
                        S_[:, 20 + s:21 + s], S_[:, 26 + s:27 + s],
                        ALU.subtract, ALU.mult),
                    deps=[trs2] + rd2, est=v_ts(D)))
            glob["stats_lock"][(ps, "b")] = tok_n2 + [t1b]
            lock3 = glob["stats_lock"].pop((ps, "c"), [])
            tok_x2l = []
            for s in range(SUB):
                tok_x2l.append(sch.emit(
                    "vector",
                    lambda e, s=s, xb=xb, o2=o2, S_=S_: e.scalar_tensor_tensor(
                        tokv(xb)[:, s, :], tokv(o2)[:, s, :], 1.0, tokv(xb)[:, s, :],
                        ALU.mult, ALU.add,
                        accum_out=S_[:, 32 + s:33 + s]),
                    deps=tok_n2 + stt["xb_readers"] + lock3, est=v_tt(1024)))
            tok_x2 = tok_x2l[-1]
            p32.give(o2, tok_x2l)
            stt["tok_x2"] = tok_x2l

            trs3, S_, rd3 = ln_stats(xt, ps, 32, tok_x2l, "c", s1_tok=tok_x2l, lock=lock3, bgrp=banksM,
                                     sub_deps=[[tok_x2l[0]], [tok_x2l[1]]])
            ln3, ln3dep = p16.take()
            tok_ln3 = []
            for s in range(SUB):
                tok_ln3.append(sch.emit(
                    "vector",
                    lambda e, s2=s, ln3=ln3, S_=S_, xt=xt: e.tensor_scalar(
                        tokv(ln3)[:, s2, :], xt[:, s2, :],
                        S_[:, 36 + s2:37 + s2], S_[:, 42 + s2:43 + s2],
                        ALU.subtract, ALU.mult),
                    deps=[trs3] + stt["tok_x2"] + ln3dep, est=v_ts(D)))
            glob["stats_lock"][(ps, "c")] = tok_ln3
            stt["x2_readers"] = rd3 + tok_ln3

            tok_evF = []
            tok_trF = []
            for s2 in range(SUB):
                tb = 0 if PT1 else (1 if PT_ROLE else s2)
                trs_ = []
                for j in range(KC):
                    trs_.append(sch.emit(
                        "tensor",
                        lambda e, tb=tb, j=j, s2=s2, ln3=ln3: e.transpose(
                            tbank(tb)[:, j * 128:(j + 1) * 128],
                            tokv(ln3)[:, s2, j * 128:(j + 1) * 128],
                            ident[:, :]),
                        deps=[tok_ln3[s2], tok_id] + tb_free[tb], est=65))
                tev = sch.emit(
                    "scalar",
                    lambda e, tb=tb, s2=s2: e.activation(
                        fF[:, 0:KC, 1 + s2 * 128: 1 + (s2 + 1) * 128],
                        tbank(tb), AF.Copy, bias=0.0, scale=1.0),
                    deps=trs_ + [glob["tok_tailF"]], est=s_act(D, True))
                tok_evF.append(tev)
                tok_trF += trs_
                tb_free[tb] = [tev]
            p16.give(ln3, tok_trF)
            stt["tok_evF"] = tok_evF

        def emit_B2b(c, stt):
            par = stt["par"]
            ps = stt["ps"]
            xb = stt["xb"]
            tok_evF = stt["tok_evF"]

            dbuf, ddep = p16.take()
            tok_d = sch.emit(
                "vector",
                lambda e, d=dbuf: e.tensor_tensor(
                    feat(d), fF[:, :, 1:CH + 1], fF[:, :, 0:CH], ALU.subtract),
                deps=tok_evF + [glob["tok_tailF"]] + ddep, est=v_tt(SUB * D, True))
            xrF, xrdep = p16.take()
            xkF, xkdep = p16.take()
            mur_bc = bass.AP(cols["mur"], 0, [[KC, 128], [1, KC], [0, CH]])
            muk_bc = bass.AP(cols["muk"], 0, [[KC, 128], [1, KC], [0, CH]])
            t_xr1 = sch.emit("vector", lambda e, d=dbuf, xr=xrF, bc=mur_bc: e.tensor_tensor(
                feat(xr), feat(d), bc, ALU.mult),
                deps=[tok_d, tok_dc] + xrdep, est=v_tt(SUB * D, True))
            tok_xr = [sch.emit("vector", lambda e, xr=xrF: e.tensor_tensor(
                feat(xr), feat(xr), fF[:, :, 0:CH], ALU.add),
                deps=[t_xr1], est=v_tt(SUB * D, True))]
            t_xk1 = sch.emit("vector", lambda e, d=dbuf, xk=xkF, bc=muk_bc: e.tensor_tensor(
                feat(xk), feat(d), bc, ALU.mult),
                deps=[tok_d] + xkdep, est=v_tt(SUB * D, True))
            tok_xk = [sch.emit("vector", lambda e, xk=xkF: e.tensor_tensor(
                feat(xk), feat(xk), fF[:, :, 0:CH], ALU.add),
                deps=[t_xk1], est=v_tt(SUB * D, True))]
            p16.give(dbuf, [t_xr1, t_xk1])
            glob["tok_tailF"] = sch.emit(
                "gpsimd", lambda e: e.tensor_copy(fF[:, :, 0:1], fF[:, :, CH:CH + 1]),
                deps=[tok_d] + tok_xr + tok_xk + [glob["tok_tailF"]], est=400)

            rf32, rfdep = (p16.take() if RF16 else p32.take())
            tok_rf = []
            all_wr = []
            for s in range(SUB):
                for half in range(2):
                    ids, bdeps = banksB.take(1)
                    b = ids[0]
                    tmm = None
                    for k in range(KC):
                        prev = [tmm] if tmm is not None else []
                        tmm = sch.emit(
                            "tensor",
                            lambda e, b=b, k=k, s=s, half=half, xr=xrF: e.matmul(
                                bank(b), feat(xr)[:, k, s * 128:(s + 1) * 128],
                                wsb["fr"][:, k, half * 512:(half + 1) * 512],
                                start=(k == 0), stop=(k == KC - 1)),
                            deps=tok_xr + wcast["fr"] + bdeps + prev, est=mm_est(512))
                    all_wr.append(tmm)
                    te = sch.emit(
                        "scalar",
                        lambda e, b=b, s=s, half=half, rf=rf32: e.activation(
                            tokv(rf)[:, s, half * 512:(half + 1) * 512], bank(b),
                            AF.Sigmoid, bias=0.0, scale=1.0),
                        deps=[tmm] + (rfdep if s == 0 and half == 0 else []),
                        est=s_act(512, True), afn="sigmoid")
                    tok_rf.append(te)
                    banksB.give(ids, [te])
            p16.give(xrF, all_wr)

            krelu, krdep = p16.take()
            tok_kr = []
            all_wk = []
            for j in range(KC // 2):
                ids, bdeps = banksB.take(1)
                b = ids[0]
                tmm = None
                pair_last = []
                for half in range(2):
                    m = 2 * j + half
                    for k in range(KC):
                        prev = [tmm] if tmm is not None else []
                        tmm = sch.emit(
                            "tensor",
                            lambda e, b=b, k=k, m=m, half=half, xk=xkF: e.matmul(
                                PS[:, b * 512 + half * CH: b * 512 + half * CH + CH],
                                wsb["fk"][:, k, m * 128:(m + 1) * 128],
                                feat(xk)[:, k, :],
                                start=(k == 0), stop=(k == KC - 1)),
                            deps=tok_xk + wcast["fk"] + bdeps + prev, est=mm_est(CH))
                    all_wk.append(tmm)
                    pair_last.append(tmm)
                te = sch.emit(
                    "scalar",
                    lambda e, b=b, j=j, kr=krelu: e.activation(
                        feat(kr)[:, 2 * j:2 * j + 2, :], bank(b, 2 * CH), AF.Relu,
                        bias=0.0, scale=1.0),
                    deps=pair_last + (krdep if j == 0 else []), est=s_act(2 * CH, True))
                tok_kr.append(te)
                banksB.give(ids, [te])
            p16.give(xkF, all_wk)
            kaF, kadep = p16.take()
            tok_ka = sch.emit("vector", lambda e, ka=kaF, kr=krelu: e.tensor_tensor(
                flat(ka), flat(kr), flat(kr), ALU.mult),
                deps=tok_kr + kadep, est=v_tt(SUB * D, True))
            p16.give(krelu, [tok_ka])

            lock4 = glob["stats_lock"].pop((ps, "d"), [])
            S_ = stats[:, ps, :]
            vout = stt["vout"]
            vscr = glob.pop("vout_scr%d" % (c % N_VOUT), [])
            tok_vo = []
            all_wv = []
            for s in range(SUB):
                for half in range(2):
                    ids, bdeps = banksB.take(1)
                    b = ids[0]
                    tmm = None
                    for k in range(KC):
                        prev = [tmm] if tmm is not None else []
                        tmm = sch.emit(
                            "tensor",
                            lambda e, b=b, k=k, s=s, half=half, ka=kaF: e.matmul(
                                bank(b), feat(ka)[:, k, s * 128:(s + 1) * 128],
                                wsb["fv"][:, k, half * 512:(half + 1) * 512],
                                start=(k == 0), stop=(k == KC - 1)),
                            deps=[tok_ka] + wcast["fv"] + bdeps + prev, est=mm_est(512))
                    all_wv.append(tmm)
                    te = sch.emit(
                        "vector",
                        lambda e, b=b, s=s, half=half, rf=rf32, S_=S_, vout=vout: e.scalar_tensor_tensor(
                            tokv(vout)[:, s, half * 512:(half + 1) * 512], bank(b),
                            1.0, tokv(rf)[:, s, half * 512:(half + 1) * 512],
                            ALU.mult, ALU.mult,
                            accum_out=S_[:, 64 + s * 2 + half:65 + s * 2 + half]),
                        deps=[tmm, tok_rf[s * 2 + half], ("DO", 16 * glob["nstores"])]
                             + glob["vout_lock"][c % N_VOUT] + vscr + lock4,
                        est=(120 + 512) / 0.96)
                    tok_vo.append(te)
                    banksB.give(ids, [te])
            p16.give(kaF, all_wv)
            (p16 if RF16 else p32).give(rf32, tok_vo)

            acc4 = stats[:, ps, 64:64 + 2 * SUB].rearrange("p (a b) -> p a b", b=2)
            t1d = sch.emit("vector", lambda e, S_=S_, acc4=acc4: e.tensor_tensor(
                S_[:, 48:48 + SUB], acc4[:, :, 0], acc4[:, :, 1], ALU.add),
                deps=tok_vo, est=150)
            trs4, S_, rd4 = ln_stats(tokv(vout), ps, 48, tok_vo, "d", s1_tok=[t1d], lock=lock4, bgrp=banksB,
                                     sub_deps=[tok_vo[0:2], tok_vo[2:4]])
            tok_n4 = []
            for s in range(SUB):
                tok_n4.append(sch.emit(
                    "vector",
                    lambda e, s=s, S_=S_, vout=vout: e.tensor_scalar(
                        tokv(vout)[:, s, :], tokv(vout)[:, s, :],
                        S_[:, 52 + s:53 + s], S_[:, 58 + s:59 + s],
                        ALU.subtract, ALU.mult),
                    deps=[trs4] + rd4, est=v_ts(D)))
            glob["stats_lock"][(ps, "d")] = tok_n4 + [t1d]
            tok_xf = sch.emit("gpsimd" if XFG else "vector",
                lambda e, xb=xb, vout=vout: e.tensor_tensor(
                    flat(xb), flat(xb), flat(vout), ALU.add),
                deps=tok_n4 + stt["x2_readers"], est=3800 if XFG else v_tt(SUB * D))
            glob["vout_lock"][c % N_VOUT] = [tok_xf]
            glob["nstores"] += 1
            sch.emit("gpsimd", dma(x_chunk_ap(c, y_h), tokv(xb)),
                     deps=[tok_xf], sem="DO", inc=16, est=5000)
            if c + NXIN < NCH:
                p2 = (c + NXIN) % NXIN
                glob["nloads"][p2] += 1
                sch.emit("sync", dma(tokv(xin[p2]), x_chunk_ap(c + NXIN, x_h)),
                         deps=[("DO", 16 * glob["nstores"])], sem="DX%d" % p2, inc=16,
                         est=5000)

        states = {}
        for r in range(NCH + 1):
            sF = emit_F_head(r) if r < NCH else None
            if r >= 1:
                emit_B1(r - 1, states[r - 1])
                emit_B2a(r - 1, states[r - 1])
            if r < NCH:
                emit_F_rest(r, sF)
                emit_F_tail(r, sF)
                states[r] = sF
            if r >= 1:
                emit_B2b(r - 1, states.pop(r - 1))

        sch.emit("sync", None, deps=[("DO", 16 * glob["nstores"])], est=100)

        streams = sch.schedule()
        t_needed = set()
        for op in sch.all:
            for d in op.deps:
                if isinstance(d, Op) and d.eng == "tensor" and op.eng != "tensor":
                    t_needed.add(d.idx)
        cnt = {k: 0 for k in ("T", "V", "S", "G")}
        dma_cnt = {}
        for e in ENGS:
            for op in streams[e]:
                if op.sem is not None:
                    dma_cnt[op.sem] = dma_cnt.get(op.sem, 0) + op.inc
                    op.inc_val = (op.sem, dma_cnt[op.sem])
                elif e in SELF_CHAIN:
                    own = ENGSEM[e]
                    cnt[own] += 1
                    op.inc_val = (own, cnt[own])
                elif e == "tensor" and op.idx in t_needed:
                    cnt["T"] += 1
                    op.inc_val = ("T", cnt["T"])

        with nc.Block() as block:
            def mk(engname):
                chain = engname in SELF_CHAIN

                def run(eng):
                    wcache = {}
                    for op in streams[engname]:
                        waits = []
                        for d in op.deps:
                            if isinstance(d, Op):
                                if d.eng == engname:
                                    continue
                                iv = d.inc_val
                                if iv is None:
                                    raise RuntimeError("dep without inc %d<-%d" % (op.idx, d.idx))
                                waits.append(iv)
                            else:
                                waits.append(d)
                        if chain and op.sem is None and op.inc_val is not None:
                            s, v = op.inc_val
                            waits.append((s, v - 1))
                        for s, v in waits:
                            if v > wcache.get(s, -1):
                                eng.wait_ge(sems[s], v)
                                wcache[s] = v
                        if op.fn is None:
                            continue
                        ins = op.fn(eng)
                        if op.inc_val is not None:
                            s, v = op.inc_val
                            ins.then_inc(sems[s], op.inc if op.sem else 1)
                return run

            block.tensor(mk("tensor"))
            block.vector(mk("vector"))
            block.scalar(mk("scalar"))
            block.gpsimd(mk("gpsimd"))
            block.sync(mk("sync"))

    return nc


_NC_CACHE = {}


def kernel(L0_attn_x, L0_attn_conv_weight, L0_attn_conv_bias, L0_attn_Wr_w,
           L0_attn_Wr_b, L0_attn_Wi_w, L0_attn_Wi_b, L0_attn_log_a,
           L0_attn_out_w, L0_attn_norm_w, L0_ffn_W_r, L0_ffn_W_k, L0_ffn_W_v,
           L0_ffn_mu_r, L0_ffn_mu_k):
    if "nc" not in _NC_CACHE:
        _NC_CACHE["nc"] = build_nc()
    nc = _NC_CACHE["nc"]

    x = np.asarray(L0_attn_x, dtype=np.float32)
    shared = {
        "wr": np.ascontiguousarray(np.asarray(L0_attn_Wr_w, dtype=np.float32).T),
        "wi": np.ascontiguousarray(np.asarray(L0_attn_Wi_w, dtype=np.float32).T),
        "ow": np.ascontiguousarray(np.asarray(L0_attn_out_w, dtype=np.float32).T),
        "fr": np.ascontiguousarray(np.asarray(L0_ffn_W_r, dtype=np.float32).T),
        "fk": np.ascontiguousarray(np.asarray(L0_ffn_W_k, dtype=np.float32).T),
        "fv": np.ascontiguousarray(np.asarray(L0_ffn_W_v, dtype=np.float32).T),
        "convw": np.ascontiguousarray(np.asarray(L0_attn_conv_weight, dtype=np.float32)[:, 0, :]),
        "convb": np.ascontiguousarray(np.asarray(L0_attn_conv_bias, dtype=np.float32)),
        "wrb": np.ascontiguousarray(np.asarray(L0_attn_Wr_b, dtype=np.float32)),
        "wib": np.ascontiguousarray(np.asarray(L0_attn_Wi_b, dtype=np.float32)),
        "loga": np.ascontiguousarray(np.asarray(L0_attn_log_a, dtype=np.float32)),
        "normw": np.ascontiguousarray(np.asarray(L0_attn_norm_w, dtype=np.float32)),
        "mur": np.ascontiguousarray(np.asarray(L0_ffn_mu_r, dtype=np.float32)),
        "muk": np.ascontiguousarray(np.asarray(L0_ffn_mu_k, dtype=np.float32)),
    }
    in_maps = []
    for b in range(8):
        m = dict(shared)
        m["x"] = np.ascontiguousarray(x[b])
        in_maps.append(m)
    res = run_bass_kernel_spmd(nc, in_maps, core_ids=list(range(8)))
    out = np.stack([np.asarray(r["y"]) for r in res.results], axis=0)
    return out.astype(np.float32)


# version 39
# speedup vs baseline: 1.0073x; 1.0054x over previous
import heapq
import numpy as np
from contextlib import ExitStack

import concourse.bass as bass
import concourse.mybir as mybir
from concourse.bass_utils import run_bass_kernel_spmd

F32 = mybir.dt.float32
BF16 = mybir.dt.bfloat16
AF = mybir.ActivationFunctionType
ALU = mybir.AluOpType
AX = mybir.AxisListType

import os
D = 1024
TOK = 4096
CH = int(os.environ.get("K_CH", "256"))
NCH = TOK // CH
KC = D // 128
SUB = CH // 128
NXIN = int(os.environ.get("K_NXIN", "2"))
NST = int(os.environ.get("K_NST", "4"))
N_P32 = int(os.environ.get("K_P32", "4"))
N_P16 = int(os.environ.get("K_P16", "8"))
N_VOUT = int(os.environ.get("K_VOUT", "1"))
BANK_MODE = os.environ.get("K_BANKS", "global")
PT_ROLE = os.environ.get("K_PTROLE", "0") == "1"
PT1 = os.environ.get("K_PT1", "0") == "1"
RF16 = os.environ.get("K_RF16", "1") == "1"
MSCAN = os.environ.get("K_MSCAN", "0") == "1"
SCAN_G = int(os.environ.get("K_SCANG", "2"))
AS16 = os.environ.get("K_AS16", "0") == "1"
HOPX = float(os.environ.get("K_HOP", "150"))
SQ2 = os.environ.get("K_SQ", "0") == "1"
XFG = os.environ.get("K_XFG", "0") == "1"
WARM_MODEL = os.environ.get("K_WARM", "1") == "1"
TBL_PICK = os.environ.get("K_TBL", "0") == "1"
LN_EPS = 1e-5
RMS_EPS = 1e-6
LOG_CLIP = float(np.log(np.float32(1e-7)))
ENGS = ("tensor", "vector", "scalar", "gpsimd", "sync")
ENGSEM = {"tensor": "T", "vector": "V", "scalar": "S", "gpsimd": "G"}
SELF_CHAIN = ("vector", "scalar", "gpsimd")
HOP_NS = 350


class Op:
    __slots__ = ("idx", "eng", "fn", "deps", "sem", "inc", "est", "inc_val", "afn")

    def __init__(self, idx, eng, fn, deps, sem, inc, est, afn):
        self.idx = idx
        self.eng = eng
        self.fn = fn
        self.deps = deps
        self.sem = sem
        self.inc = inc
        self.est = est
        self.inc_val = None
        self.afn = afn


TBL_NS = 1283.0
TBL_BONUS = float(os.environ.get("K_TBLNS", "2600"))
TBL_WIN = int(os.environ.get("K_TBLWIN", "10"))
N_JIT = int(os.environ.get("K_NJIT", "0"))
JIT_NS = float(os.environ.get("K_JITNS", "900"))
COLD_IDLE = float(os.environ.get("K_CIDLE", "1500"))
COLD_WIN = 3400.0


class Sched:
    def __init__(self):
        self.all = []
        self.base = {}
        self.dma_val = {}
        self.dma_map = {}

    def emit(self, eng, fn, deps=(), sem=None, inc=0, est=600, afn=None):
        deps = [d for d in deps if d is not None] + self.base.get(eng, [])
        if eng in SELF_CHAIN:
            est += 150.0
        op = Op(len(self.all), eng, fn, list(deps), sem, inc, est, afn)
        self.all.append(op)
        if sem is not None:
            self.dma_val[sem] = self.dma_val.get(sem, 0) + inc
            self.dma_map[(sem, self.dma_val[sem])] = op
        return op

    def schedule(self):
        global HOP_NS
        HOP_NS = HOPX
        n = len(self.all)
        by_sem = {}
        for (s, v), prod in self.dma_map.items():
            by_sem.setdefault(s, []).append((v, prod))
        for s in by_sem:
            by_sem[s].sort(key=lambda t: t[0])
        for op in self.all:
            extra = []
            for d in op.deps:
                if not isinstance(d, Op):
                    s, v = tuple(d)
                    for cv, prod in by_sem.get(s, ()):
                        if cv <= v:
                            extra.append(prod)
                        else:
                            break
            op.deps += extra
        succ = [[] for _ in range(n)]
        npred = [0] * n
        for op in self.all:
            for d in op.deps:
                if isinstance(d, Op):
                    succ[d.idx].append(op.idx)
                    npred[op.idx] += 1
        prio0 = [0.0] * n
        for i in range(n - 1, -1, -1):
            p = 0.0
            for s in succ[i]:
                if prio0[s] > p:
                    p = prio0[s]
            prio0[i] = p + self.all[i].est
        best_res = None
        rng = np.random.RandomState(12345)
        for trial in range(N_JIT + 1):
            if trial == 0:
                prio = prio0
            else:
                prio = [p + float(j) for p, j in
                        zip(prio0, rng.uniform(0.0, JIT_NS, size=n))]
            res = self._sim(n, succ, npred, prio)
            if best_res is None or res[0] < best_res[0]:
                best_res = res
        makespan, streams, finish = best_res
        self.finish = finish
        return streams

    def _sim(self, n, succ, npred, prio):
        eng_free = {e: 0.0 for e in ENGS}
        finish = [0.0] * n
        ready = {e: [] for e in ENGS}
        pend = list(npred)

        def push(i):
            op = self.all[i]
            key = (op.idx, i) if op.eng == "sync" else (-prio[i], i)
            heapq.heappush(ready[op.eng], key)

        for i in range(n):
            if pend[i] == 0:
                push(i)
        streams = {e: [] for e in ENGS}
        remaining = n
        cur_tbl = None
        pe_last_end = -1e18
        pe_cold_until = -1e18

        def pick_scalar():
            tmp = []
            while ready["scalar"] and len(tmp) < TBL_WIN:
                tmp.append(heapq.heappop(ready["scalar"]))
            best_key, best_adj = None, None
            for key in tmp:
                op = self.all[key[-1]]
                adj = -key[0] + (TBL_BONUS if (op.afn and op.afn == cur_tbl) else 0.0)
                if best_adj is None or adj > best_adj:
                    best_adj, best_key = adj, key
            for key in tmp:
                if key is not best_key:
                    heapq.heappush(ready["scalar"], key)
            return best_key

        while remaining:
            best = None
            scalar_key = None
            for e in ENGS:
                if not ready[e]:
                    continue
                if e == "scalar" and TBL_PICK:
                    scalar_key = pick_scalar()
                    i = scalar_key[-1]
                else:
                    i = ready[e][0][-1]
                op = self.all[i]
                dep_t = 0.0
                for d in op.deps:
                    if isinstance(d, Op):
                        t = finish[d.idx]
                        if d.eng != op.eng:
                            t += HOP_NS
                        if t > dep_t:
                            dep_t = t
                start = max(eng_free[e], dep_t)
                if best is None or start < best[0]:
                    best = (start, e, i)
            start, e, i = best
            if e != "scalar":
                if scalar_key is not None:
                    heapq.heappush(ready["scalar"], scalar_key)
                heapq.heappop(ready[e])
            elif scalar_key is None:
                heapq.heappop(ready[e])
            op = self.all[i]
            dur = op.est
            if e == "scalar" and op.afn:
                if op.afn != cur_tbl:
                    dur += TBL_NS
                    cur_tbl = op.afn
            if e == "tensor" and WARM_MODEL:
                if start - pe_last_end > COLD_IDLE:
                    pe_cold_until = start + COLD_WIN
                if start < pe_cold_until:
                    dur = dur * 2.0
                pe_last_end = start + dur
            finish[i] = start + dur
            eng_free[e] = finish[i]
            streams[e].append(op)
            remaining -= 1
            for s in succ[i]:
                pend[s] -= 1
                if pend[s] == 0:
                    push(s)
        return max(finish), streams, finish


class Banks:
    def __init__(self, base, nb):
        self.base = base
        self.nb = nb
        self.free = {}
        self.ptr = 0

    def take(self, n):
        if self.ptr + n > self.nb:
            self.ptr = 0
        ids = [self.base + self.ptr + i for i in range(n)]
        self.ptr = (self.ptr + n) % self.nb
        deps = [t for i in ids for t in self.free.get(i, [])]
        return ids, deps

    def give(self, ids, toks):
        for i in ids:
            self.free[i] = list(toks)


class Pool:
    def __init__(self, slots):
        self.q = [(h, []) for h in slots]

    def take(self):
        best = min(range(len(self.q)),
                   key=lambda i: max([t.idx for t in self.q[i][1]
                                      if isinstance(t, Op)], default=-1))
        return self.q.pop(best)

    def give(self, h, toks):
        self.q.append((h, list(toks)))


def v_tt(f, bf=False):
    return (151 + (f // 2 if bf else f)) / 0.96


def v_ts(f, bf=False):
    return (58 + (f // 4 if bf else f // 2)) / 0.96


def v_red(f):
    return (58 + f) / 0.96


def s_act(f, psum=False):
    return ((330 if psum else 280) + f) / 1.2


def mm_est(nn):
    return nn / 2.4 + (12 if WARM_MODEL else 45)


def build_nc():
    nc = bass.Bass("TRN2", target_bir_lowering=False)

    x_h = nc.dram_tensor("x", [TOK, D], F32, kind="ExternalInput")
    y_h = nc.dram_tensor("y", [TOK, D], F32, kind="ExternalOutput")
    wnames = ("wr", "wi", "ow", "fr", "fk", "fv")
    w_h = {n: nc.dram_tensor(n, [D, D], F32, kind="ExternalInput") for n in wnames}
    convw_h = nc.dram_tensor("convw", [D, 4], F32, kind="ExternalInput")
    vecnames = ("convb", "wrb", "wib", "loga", "mur", "muk")
    v_h = {n: nc.dram_tensor(n, [D], F32, kind="ExternalInput") for n in vecnames}
    normw_h = nc.dram_tensor("normw", [D], F32, kind="ExternalInput")

    sch = Sched()
    if BANK_MODE == "global":
        banksF = banksM = banksB = Banks(0, 7 if PT1 else 6)
    else:
        banksF = Banks(0, 2)
        banksM = Banks(2, 2)
        banksB = Banks(4, 2)

    with ExitStack() as st:
        def sb(name, shape, dt):
            return st.enter_context(nc.sbuf_tensor(name, shape, dt))

        sems = {}
        for sname in (("T", "V", "S", "G", "DC", "DO", "DB") + tuple("DW%d" % i for i in range(6))
                      + tuple("DX%d" % i for i in range(NXIN))):
            sems[sname] = st.enter_context(nc.semaphore(sname))

        _wshrink = os.environ.get("K_WSHRINK", "0") == "1"
        if _wshrink:
            _wshared = sb("w_shared", [128, KC, D], BF16)
            wsb = {n: _wshared for n in wnames}
        else:
            wsb = {n: sb("w_" + n, [128, KC, D], BF16) for n in wnames}
        ident = sb("ident", [128, 128], BF16)
        diag = sb("diag", [128, 32, 128], BF16)
        ones16 = sb("ones16", [128, CH], BF16)
        wconv = sb("wconv", [128, KC, 4], F32)
        cols = {n: sb("c_" + n, [128, KC], F32) for n in vecnames}
        gbias = sb("gbias", [65, D], BF16)
        lnsig = cols["loga"]
        constvals = [1e-5, 1e-6, 1.0, 2.0, -1.0, 1.0 / D]
        cdb = sb("cdb", [128, len(constvals)], F32)
        for i, v in enumerate(constvals):
            nc.const_aps.register(F32, float(v), cdb[:, i:i + 1])
        normw_rep = sb("normw_rep", [128, D], BF16)
        zeros32 = sb("zeros32", [128, CH], F32)
        ln1F = sb("ln1F", [128, KC, CH + 3], BF16)
        fF = sb("fF", [128, KC, CH + 1], BF16)
        scanmask = sb("scanmask", [128, SUB * D], BF16) if MSCAN else None
        carryC = sb("carryC", [128, KC], F32)
        carryS = sb("carryS", [128, KC], F32)
        ctmp = sb("ctmp", [128, KC], F32)
        stmp = sb("stmp", [128, KC], F32)
        stats = sb("stats", [128, NST, 96], F32)
        xin = [sb("xin%d" % i, [128, SUB * D], F32) for i in range(NXIN)]
        vouts = [sb("vout%d" % i, [128, SUB * D], F32) for i in range(N_VOUT)]
        p32 = Pool([sb("p32_%d" % i, [128, SUB * D], F32) for i in range(N_P32)])
        p16 = Pool([sb("p16_%d" % i, [128, SUB * D], BF16) for i in range(N_P16)])
        PS = st.enter_context(nc.psum_tensor("PS", [128, 3584 if PT1 else 3072], F32))
        PT = st.enter_context(nc.psum_tensor("PT", [128, 1024 if PT1 else 2048], BF16))
        tb_free = [[], []]

        def bank(b, n=512):
            return PS[:, b * 512: b * 512 + n]

        def bank2(b):
            return PS[:, b * 512: b * 512 + 1024]

        def tbank(t):
            return PT[:, t * 1024:(t + 1) * 1024]

        def feat(h, c=KC):
            return h[:, :].rearrange("p (c t) -> p c t", c=c)

        def tokv(h):
            return h[:, :].rearrange("p (s d) -> p s d", s=SUB)

        def flat(h):
            return h[:, :]

        def dma(dst, src):
            return lambda eng, o=dst, i=src: eng.dma_start(o, i)

        def dma_nc(dst, src):
            return lambda eng, o=dst, i=src: eng.dma_start(o, i, allow_slow_non_contiguous=True)

        ndc = 0
        for name in ("loga", "mur", "muk", "wrb", "wib"):
            sch.emit("sync", dma_nc(cols[name][:, :], v_h[name][:].rearrange("(k p) -> p k", p=128)),
                     sem="DC", inc=16, est=3000)
            ndc += 1
        sch.emit("sync", dma(wconv[:, :, :], convw_h[:, :].rearrange("(k p) t -> p k t", p=128)),
                 sem="DC", inc=16, est=3000)
        ndc += 1
        sch.emit("gpsimd", dma(normw_rep[0:1, :], normw_h[:].rearrange("(a d) -> a d", a=1)),
                 sem="DB", inc=16, est=3000)
        tok_dc = ("DC", 16 * ndc)
        tok_db = ("DB", 32)
        tok_dcg = tok_dc
        sch.emit("gpsimd", dma(gbias[64:65, :], v_h["convb"][:].rearrange("(a d) -> a d", a=1)),
                 sem="DB", inc=16, est=2000)

        wcast = {}
        for wi_, n in enumerate(wnames):
            wcast[n] = [sch.emit(
                "gpsimd",
                dma(wsb[n][:, :, :], w_h[n][:, :].rearrange("(k p) d -> p k d", p=128)),
                sem="DW%d" % wi_, inc=16, est=12000)]

        def x_chunk_ap(c, h):
            return h[c * CH:(c + 1) * CH, :].rearrange("(s p) d -> p s d", p=128)

        tok_cdb = None
        for i, v in enumerate(constvals):
            tok_cdb = sch.emit("gpsimd", lambda e, i=i, v=v: e.memset(cdb[:, i:i + 1], float(v)),
                               deps=[tok_cdb], est=300)
        sch.base["scalar"] = [tok_cdb]
        tok_i1 = sch.emit("gpsimd", lambda e: e.iota(
            zeros32[:, 0:128], [[1, 128]], channel_multiplier=-1, allow_small_or_imprecise_dtypes=True),
            est=400)
        tok_id = sch.emit("vector", lambda e: e.tensor_single_scalar(ident[:, :], zeros32[:, 0:128], 0.0, ALU.is_equal),
                          deps=[tok_i1], est=300)
        tok_diag = []
        for ch in range(KC):
            for kk in range(4):
                scl = wconv[:, ch, kk:kk + 1]
                tok_diag.append(sch.emit(
                    "scalar",
                    lambda e, i=ch * 4 + kk, scl=scl: e.activation(
                        diag[:, i, :], ident[:, :], AF.Copy,
                        bias=0.0, scale=scl),
                    deps=[tok_id, tok_dc, tok_cdb], est=300))
        tok_ones16 = sch.emit("gpsimd", lambda e: e.memset(ones16[:, :], 1.0), est=300)
        t_sig = sch.emit("scalar", lambda e: e.activation(lnsig[:, :], cols["loga"][:, :], AF.Sigmoid, bias=0.0, scale=1.0),
                         deps=[tok_dc], est=300)
        tok_ln0 = sch.emit("scalar", lambda e: e.activation(lnsig[:, :], lnsig[:, :], AF.Ln, bias=0.0, scale=1.0),
                           deps=[t_sig], est=300)
        tok_lnsig = sch.emit("vector", lambda e: e.tensor_single_scalar(lnsig[:, :], lnsig[:, :], 8.0, ALU.mult),
                             deps=[tok_ln0], est=200)
        tok_nw = []
        for half in range(2):
            ids, bdeps = banksM.take(1)
            b = ids[0]
            tmb = sch.emit(
                "tensor",
                lambda e, b=b, half=half: e.matmul(
                    bank(b), ones16[0:1, 0:128],
                    normw_rep[0:1, half * 512:(half + 1) * 512],
                    start=True, stop=True),
                deps=[tok_ones16, tok_db] + bdeps, est=2000)
            ev = sch.emit(
                "scalar",
                lambda e, b=b, half=half: e.activation(
                    normw_rep[:, half * 512:(half + 1) * 512], bank(b), AF.Copy,
                    bias=0.0, scale=1.0),
                deps=[tmb], est=s_act(512, True))
            tok_nw.append(ev)
            banksM.give(ids, [ev])
        tok_z32 = sch.emit("gpsimd", lambda e: e.memset(zeros32[:, :], 0.0),
                           deps=[tok_id], est=400)
        if MSCAN:
            tok_mask = sch.emit("gpsimd", lambda e: e.memset(scanmask[:, :], 1.0), est=400)
            tok_mask = sch.emit("gpsimd", lambda e: e.memset(feat(scanmask)[:, :, 0], 0.0),
                                deps=[tok_mask], est=300)
        else:
            tok_mask = None
        tok_halo1 = sch.emit("gpsimd", lambda e: e.memset(ln1F[:, :, 0:3], 0.0), est=300)
        tok_cc0 = sch.emit("gpsimd", lambda e: e.memset(carryC[:, :], 0.0), est=200)
        tok_cs0 = sch.emit("gpsimd", lambda e: e.memset(carryS[:, :], 0.0), est=200)
        glob_carry = {"C": tok_cc0, "S": tok_cs0}
        tok_haloF = sch.emit("gpsimd", lambda e: e.memset(fF[:, :, 0:1], 0.0), est=300)

        glob = {
            "nloads": [0] * NXIN, "nstores": 0,
            "xin_free": [[] for _ in range(NXIN)],
            "tok_tail1": tok_halo1, "tok_tailF": tok_haloF,
            "stats_lock": {},
            "vout_lock": [[] for _ in range(N_VOUT)],
        }

        for c in range(min(NXIN, NCH)):
            par = c % NXIN
            glob["nloads"][par] += 1
            sch.emit("sync", dma(tokv(xin[par]), x_chunk_ap(c, x_h)),
                     deps=glob["xin_free"][par], sem="DX%d" % par, inc=16,
                     est=5000)

        def ln_stats(xap, par, base, src_deps, tag, s1_tok=None, lock=None, bgrp=None,
                     sub_deps=None):
            banks = bgrp
            S_ = stats[:, par, :]
            if lock is None:
                lock = glob["stats_lock"].pop((par, tag), [])
            if s1_tok is None:
                t1 = [sch.emit("vector", lambda e, S_=S_, xap=xap, base=base: e.tensor_reduce(
                    S_[:, base:base + SUB], xap, AX.X, ALU.add),
                    deps=src_deps + lock, est=v_red(SUB * D))]
            else:
                t1 = list(s1_tok)
            ids, bdeps = banks.take(2)
            b = ids[0]
            tsq = []
            for s in range(SUB):
                sd = sub_deps[s] if sub_deps is not None else src_deps
                tsq.append(sch.emit(
                    "scalar",
                    lambda e, s2=s, b=b, S_=S_, xap=xap, base=base: e.activation(
                        bank2(b), xap[:, s2, :], AF.Square, bias=0.0, scale=1.0,
                        accum_out=S_[:, base + 2 + s2:base + 3 + s2]),
                    deps=sd + bdeps + lock, est=s_act(D)))
            banks.give(ids, tsq)
            tm_ = sch.emit("vector", lambda e, S_=S_, base=base: e.tensor_single_scalar(
                S_[:, base + 4:base + 4 + SUB], S_[:, base:base + SUB], 1.0 / D, ALU.mult),
                deps=t1, est=150)
            tmm = sch.emit("vector", lambda e, S_=S_, base=base: e.tensor_tensor(
                S_[:, base + 8:base + 8 + SUB], S_[:, base + 4:base + 4 + SUB], S_[:, base + 4:base + 4 + SUB], ALU.mult),
                deps=[tm_], est=150)
            tq = sch.emit("vector", lambda e, S_=S_, base=base: e.scalar_tensor_tensor(
                S_[:, base + 8:base + 8 + SUB], S_[:, base + 2:base + 2 + SUB], 1.0 / D,
                S_[:, base + 8:base + 8 + SUB], ALU.mult, ALU.subtract),
                deps=tsq + [tmm], est=150)
            tln = sch.emit("scalar", lambda e, S_=S_, base=base: e.activation(
                S_[:, base + 14:base + 14 + SUB], S_[:, base + 8:base + 8 + SUB], AF.Ln, bias=LN_EPS, scale=1.0),
                deps=[tq], est=250, afn="exp")
            trs = sch.emit("scalar", lambda e, S_=S_, base=base: e.activation(
                S_[:, base + 10:base + 10 + SUB], S_[:, base + 14:base + 14 + SUB], AF.Exp, bias=0.0, scale=-0.5),
                deps=[tln], est=250, afn="exp")
            return trs, S_, t1 + tsq

        def emit_F_head(c):
            par = c % NXIN
            ps = c % NST
            xb = xin[par]
            xt = tokv(xb)
            dxs = "DX%d" % par
            tok_x = (dxs, 16 * glob["nloads"][par])
            stt = {"par": par, "ps": ps, "xb": xb, "xt": xt, "tok_x": tok_x}

            trs1, S_, rd1 = ln_stats(xt, ps, 0, [tok_x], "a", bgrp=banksF)
            ln1, ln1dep = p16.take()
            tok_ln1 = []
            for s in range(SUB):
                tok_ln1.append(sch.emit(
                    "vector",
                    lambda e, s2=s, S_=S_, ln1=ln1, xt=xt: e.tensor_scalar(
                        tokv(ln1)[:, s2, :], xt[:, s2, :],
                        S_[:, 4 + s2:5 + s2], S_[:, 10 + s2:11 + s2],
                        ALU.subtract, ALU.mult),
                    deps=[trs1, tok_x] + ln1dep, est=v_ts(D)))
            glob["stats_lock"][(ps, "a")] = tok_ln1
            stt["xb_readers"] = rd1 + tok_ln1

            tok_tr1 = []
            tok_ev1 = []
            for s2 in range(SUB):
                tb = 0 if (PT1 or PT_ROLE) else s2
                trs_ = []
                for j in range(KC):
                    trs_.append(sch.emit(
                        "tensor",
                        lambda e, tb=tb, j=j, s2=s2, ln1=ln1: e.transpose(
                            tbank(tb)[:, j * 128:(j + 1) * 128],
                            tokv(ln1)[:, s2, j * 128:(j + 1) * 128],
                            ident[:, :]),
                        deps=[tok_ln1[s2], tok_id] + tb_free[tb], est=65))
                tev = sch.emit(
                    "scalar",
                    lambda e, tb=tb, s2=s2: e.activation(
                        ln1F[:, 0:KC, 3 + s2 * 128: 3 + (s2 + 1) * 128],
                        tbank(tb), AF.Copy, bias=0.0, scale=1.0),
                    deps=trs_ + [glob["tok_tail1"]], est=s_act(D, True))
                tok_ev1.append(tev)
                tok_tr1 += trs_
                tb_free[tb] = [tev]
            p16.give(ln1, tok_tr1)
            stt["tok_ev1"] = tok_ev1
            return stt

        def emit_F_rest(c, stt):
            tok_ev1 = stt["tok_ev1"]
            xcFh, xcFdep = p16.take()
            tok_xcF = []
            tmm_pair = []
            for pr in range(KC // 2):
                ids, bdeps = banksF.take(1)
                b = ids[0]
                tm = None
                for half in range(2):
                    ch = pr * 2 + half
                    for kk in range(4):
                        prev = [tm] if tm is not None else []
                        tm = sch.emit(
                            "tensor",
                            lambda e, b=b, ch=ch, kk=kk, half=half: e.matmul(
                                PS[:, b * 512 + half * CH: b * 512 + half * CH + CH],
                                diag[:, ch * 4 + kk, :], ln1F[:, ch, kk:kk + CH],
                                start=(kk == 0), stop=False),
                            deps=tok_ev1 + [tok_diag[ch * 4 + kk]] + bdeps + prev
                                 + [glob["tok_tail1"]],
                            est=mm_est(CH))
                    tm = sch.emit(
                        "tensor",
                        lambda e, b=b, ch=ch, half=half: e.matmul(
                            PS[:, b * 512 + half * CH: b * 512 + half * CH + CH],
                            gbias[64:65, ch * 128:(ch + 1) * 128], ones16[64:65, :],
                            start=False, stop=True),
                        deps=[tm, tok_db, tok_ones16], est=mm_est(CH))
                tmm_pair.append(tm)
                t2 = sch.emit(
                    "scalar",
                    lambda e, b=b, pr=pr: e.activation(
                        feat(xcFh)[:, pr * 2:pr * 2 + 2, :], bank(b, 2 * CH), AF.Copy,
                        bias=0.0, scale=1.0),
                    deps=[tm] + (xcFdep if pr == 0 else []), est=s_act(512, True))
                tok_xcF.append(t2)
                banksF.give([b], [t2])
            xcF = xcFh
            glob["tok_tail1"] = sch.emit(
                "gpsimd", lambda e: e.tensor_copy(ln1F[:, :, 0:3], ln1F[:, :, CH:CH + 3]),
                deps=tmm_pair + [glob["tok_tail1"]], est=500)

            r16, r16dep = p16.take()
            i16, i16dep = p16.take()
            all_gate_mm = []
            for wn, biasn, gb, gdep in (("wr", "wrb", r16, r16dep),
                                        ("wi", "wib", i16, i16dep)):
                toks = []
                for m in range(KC):
                    ids, bdeps = banksF.take(1)
                    b = ids[0]
                    tmm = None
                    for k in range(KC):
                        prev = [tmm] if tmm is not None else []
                        tmm = sch.emit(
                            "tensor",
                            lambda e, b=b, wn=wn, k=k, m=m, xcF=xcF: e.matmul(
                                bank(b, CH), wsb[wn][:, k, m * 128:(m + 1) * 128],
                                feat(xcF)[:, k, :],
                                start=(k == 0), stop=(k == KC - 1)),
                            deps=tok_xcF + wcast[wn] + bdeps + prev,
                            est=mm_est(CH))
                    all_gate_mm.append(tmm)
                    te = sch.emit(
                        "scalar",
                        lambda e, b=b, gb=gb, m=m, biasn=biasn: e.activation(
                            feat(gb)[:, m, :], bank(b, CH), AF.Sigmoid,
                            bias=cols[biasn][:, m:m + 1], scale=1.0),
                        deps=[tmm, tok_dcg] + (gdep if m == 0 else []),
                        est=s_act(CH, True), afn="sigmoid")
                    toks.append(te)
                    banksF.give(ids, [te])
                stt[wn + "_toks"] = toks
            stt["xcF"] = xcF
            stt["tok_xcF"] = tok_xcF
            stt["r16"] = r16
            stt["i16"] = i16
            stt["gate_mm"] = all_gate_mm

        NSPL = int(os.environ.get("K_NSPL", "1"))
        NHF = SUB * D // NSPL
        CPS = KC // NSPL

        def half(h, j):
            return h[:, j * NHF:(j + 1) * NHF]

        def emit_F_tail(c, stt):
            z, zdep = p32.take()
            tok_z = []
            for j in range(NSPL):
                bc = bass.AP(lnsig, CPS * j, [[KC, 128], [1, CPS], [0, CH]])
                tok_z.append(sch.emit(
                    "vector",
                    lambda e, j=j, z=z, r16=stt["r16"], bc=bc: e.tensor_tensor(
                        feat(half(z, j), CPS), feat(half(r16, j), CPS), bc, ALU.mult),
                    deps=stt["wr_toks"][CPS * j:CPS * (j + 1)] + [tok_lnsig]
                         + (zdep if j == 0 else []), est=v_tt(NHF, True)))
            p16.give(stt["r16"], tok_z)
            tbuf, tdep = p16.take()
            tok_t = []
            for j in range(NSPL):
                tok_t.append(sch.emit(
                    "vector",
                    lambda e, j=j, t=tbuf, i16=stt["i16"], xcF=stt["xcF"]: e.tensor_tensor(
                        half(t, j), half(i16, j), half(xcF, j), ALU.mult),
                    deps=stt["wi_toks"][CPS * j:CPS * (j + 1)] + stt["tok_xcF"][(CPS // 2) * j:(CPS // 2) * (j + 1)]
                         + (tdep if j == 0 else []), est=v_tt(NHF, True)))
            p16.give(stt["i16"], tok_t)
            p16.give(stt["xcF"], tok_t + stt["gate_mm"])
            stt["z"] = z
            stt["tok_z"] = tok_z
            stt["t"] = tbuf
            stt["tok_t"] = tok_t

        def emit_B1(c, stt):
            z, tbuf = stt["z"], stt["t"]

            la, ladep = p32.take()
            tok_la = []
            for j in range(NSPL):
                tok_la.append(sch.emit(
                    "vector", lambda e, j=j, la=la, z=z: e.tensor_single_scalar(
                        half(la, j), half(z, j), LOG_CLIP, ALU.max),
                    deps=[stt["tok_z"][j]] + (ladep if j == 0 else []), est=v_ts(NHF)))
            sbuf_s, sdep = p32.take()
            tok_s = []
            for j in range(NSPL):
                tok_s.append(sch.emit(
                    "scalar", lambda e, j=j, s=sbuf_s, z=z: e.activation(
                        half(s, j), half(z, j), AF.Exp, bias=0.0, scale=2.0),
                    deps=[stt["tok_z"][j]] + (sdep if j == 0 else []),
                    est=s_act(NHF), afn="exp"))
            p32.give(z, tok_s + tok_la)
            tok_sp = [sch.emit(
                "vector", lambda e, j=j, s=sbuf_s: e.tensor_single_scalar(
                    half(s, j), half(s, j), 1.0 - 1e-6, ALU.min),
                deps=[tok_s[j]], est=v_ts(NHF)) for j in range(NSPL)]

            sq16, sqdep16 = p16.take()
            tok_sq = []
            for j in range(NSPL):
                if SQ2:
                    tok_lnw = sch.emit("scalar", lambda e, j=j, s=sbuf_s: e.activation(
                        half(s, j), half(s, j), AF.Ln, bias=1.0, scale=-1.0),
                        deps=[tok_sp[j]], est=s_act(NHF), afn="exp")
                    tok_sq.append(sch.emit(
                        "scalar", lambda e, j=j, s=sbuf_s, sq=sq16: e.activation(
                            half(sq, j), half(s, j), AF.Exp, bias=0.0, scale=0.5),
                        deps=[tok_lnw] + (sqdep16 if j == 0 else []),
                        est=s_act(NHF), afn="exp"))
                else:
                    tok_sq.append(sch.emit(
                        "scalar", lambda e, j=j, s=sbuf_s, sq=sq16: e.activation(
                            half(sq, j), half(s, j), AF.Sqrt, bias=1.0, scale=-1.0),
                        deps=[tok_sp[j]] + (sqdep16 if j == 0 else []),
                        est=s_act(NHF), afn="sqrt"))
            p32.give(sbuf_s, tok_sq)
            tok_u1 = [sch.emit(
                "vector", lambda e, j=j, t=tbuf, sq=sq16: e.tensor_tensor(
                    half(t, j), half(sq, j), half(t, j), ALU.mult),
                deps=[tok_sq[j], stt["tok_t"][j]], est=v_tt(NHF, True))
                for j in range(NSPL)]
            p16.give(sq16, tok_u1)
            cbuf, cdep = p32.take()
            if MSCAN:
                W = SCAN_G * CH
                tok_c = [sch.emit(
                    "vector",
                    lambda e, j=j, cb=cbuf, la=la: e.tensor_tensor_scan(
                        flat(cb)[:, j * W:(j + 1) * W], flat(scanmask)[:, j * W:(j + 1) * W],
                        flat(la)[:, j * W:(j + 1) * W],
                        0.0, ALU.mult, ALU.add),
                    deps=[tok_mask] + tok_la + cdep, est=(400 + W) / 0.96)
                    for j in range(KC // SCAN_G)]
            else:
                tok_c = []
                for m in range(KC):
                    tok_c.append(sch.emit(
                        "vector",
                        lambda e, m=m, cb=cbuf, la=la: e.tensor_tensor_scan(
                            feat(cb)[:, m, :], feat(la)[:, m, :], zeros32[:, :],
                            0.0, ALU.add, ALU.add),
                        deps=[tok_z32] + tok_la + cdep, est=(400 + CH) / 0.96))
            p32.give(la, tok_c)
            prevC = glob_carry["C"]
            tlast = sch.emit("vector", lambda e, cb=cbuf: e.tensor_copy(
                ctmp[:, :], feat(cb)[:, :, CH - 1]), deps=tok_c, est=150)
            tok_m_l = []
            for m in range(KC):
                tok_m_l.append(sch.emit(
                    "vector",
                    lambda e, m=m, cb=cbuf: e.tensor_scalar(
                        feat(cb)[:, m, :], feat(cb)[:, m, :],
                        carryC[:, m:m + 1], -80.0, ALU.add, ALU.max),
                    deps=[tlast, prevC] + tok_c, est=v_ts(CH)))
            glob_carry["C"] = sch.emit(
                "vector", lambda e: e.tensor_tensor(
                    carryC[:, :], carryC[:, :], ctmp[:, :], ALU.add),
                deps=[tlast, prevC] + tok_m_l, est=150)
            tok_m = tok_m_l[-1]
            Ainv, Aidep = p16.take()
            tok_Ai = [sch.emit("scalar", lambda e, j=j, A=Ainv, cb=cbuf: e.activation(
                half(A, j), half(cb, j), AF.Exp, bias=0.0, scale=-1.0),
                deps=tok_m_l[CPS * j:CPS * (j + 1)] + (Aidep if j == 0 else []),
                est=s_act(NHF), afn="exp") for j in range(NSPL)]
            tok_u = [sch.emit("vector", lambda e, j=j, t=tbuf, A=Ainv: e.tensor_tensor(
                half(t, j), half(t, j), half(A, j), ALU.mult),
                deps=[tok_Ai[j], tok_u1[j]], est=v_tt(NHF, True)) for j in range(NSPL)]
            p16.give(Ainv, tok_u)
            Ssc, Sdep = (p16.take() if AS16 else p32.take())
            if MSCAN:
                W = SCAN_G * CH
                tok_S = [sch.emit(
                    "vector",
                    lambda e, j=j, S2=Ssc, t=tbuf: e.tensor_tensor_scan(
                        flat(S2)[:, j * W:(j + 1) * W], flat(scanmask)[:, j * W:(j + 1) * W],
                        flat(t)[:, j * W:(j + 1) * W],
                        0.0, ALU.mult, ALU.add),
                    deps=[tok_mask] + tok_u + Sdep, est=(400 + W) / 0.96)
                    for j in range(KC // SCAN_G)]
            else:
                tok_S = []
                for m in range(KC):
                    tok_S.append(sch.emit(
                        "vector",
                        lambda e, m=m, S2=Ssc, t=tbuf: e.tensor_tensor_scan(
                            feat(S2)[:, m, :], feat(t)[:, m, :], zeros32[:, :],
                            0.0, ALU.add, ALU.add),
                        deps=[tok_z32] + tok_u + Sdep, est=(400 + CH // 2) / 0.96))
            p16.give(tbuf, tok_S)
            prevS = glob_carry["S"]
            slast = sch.emit("vector", lambda e, S2=Ssc: e.tensor_copy(
                stmp[:, :], feat(S2)[:, :, CH - 1]), deps=tok_S, est=150)
            Abuf, Adep = (p16.take() if AS16 else p32.take())
            tok_A = [sch.emit("scalar", lambda e, j=j, A=Abuf, cb=cbuf: e.activation(
                half(A, j), half(cb, j), AF.Exp, bias=0.0, scale=1.0),
                deps=tok_m_l[CPS * j:CPS * (j + 1)] + (Adep if j == 0 else []),
                est=s_act(NHF), afn="exp") for j in range(NSPL)]
            p32.give(cbuf, tok_A + tok_Ai)
            hF, hdep = p16.take()
            tok_h_l = []
            for m in range(KC):
                tok_h_l.append(sch.emit(
                    "vector",
                    lambda e, m=m, h=hF, A=Abuf, S2=Ssc: e.scalar_tensor_tensor(
                        feat(h)[:, m, :], feat(S2)[:, m, :], carryS[:, m:m + 1],
                        feat(A)[:, m, :], ALU.add, ALU.mult),
                    deps=tok_A + [prevS] + tok_S + hdep,
                    est=(151 + (CH // 2 if AS16 else CH)) / 0.96))
            glob_carry["S"] = sch.emit(
                "vector", lambda e: e.tensor_tensor(
                    carryS[:, :], carryS[:, :], stmp[:, :], ALU.add),
                deps=[slast, prevS] + tok_h_l, est=150)
            tok_h = tok_h_l[-1]
            (p16 if AS16 else p32).give(Abuf, tok_h_l)
            (p16 if AS16 else p32).give(Ssc, tok_h_l)
            stt["hF"] = hF
            stt["tok_h"] = tok_h_l

        def emit_B2a(c, stt):
            par = stt["par"]
            ps = stt["ps"]
            xb = stt["xb"]
            xt = stt["xt"]
            hF = stt["hF"]
            tok_h_l = stt["tok_h"]

            vout = vouts[c % N_VOUT]
            stt["vout"] = vout
            o2, o2dep = p32.take()
            S_ = stats[:, ps, :]
            rmslock = glob["stats_lock"].pop((ps, "rms%d" % (c % N_VOUT)), [])
            lock2 = glob["stats_lock"].pop((ps, "b"), [])
            tok_o2 = []
            all_ow = []
            for s in range(SUB):
                ids, bdeps = banksM.take(2)
                tok_half = []
                for half in range(2):
                    b = ids[half]
                    tmm = None
                    for k in range(KC):
                        prev = [tmm] if tmm is not None else []
                        tmm = sch.emit(
                            "tensor",
                            lambda e, b=b, k=k, s=s, half=half, hF=hF: e.matmul(
                                bank(b),
                                feat(hF)[:, k, s * 128:(s + 1) * 128],
                                wsb["ow"][:, k, half * 512:(half + 1) * 512],
                                start=(k == 0), stop=(k == KC - 1)),
                            deps=tok_h_l + wcast["ow"] + bdeps + prev, est=mm_est(512))
                    tok_half.append(tmm)
                all_ow += tok_half
                tsq = []
                for half in range(2):
                    tsq.append(sch.emit(
                        "scalar",
                        lambda e, b=ids[half], s=s, half=half, S_=S_, vout=vout: e.activation(
                            tokv(vout)[:, s, half * 512:(half + 1) * 512], bank(b),
                            AF.Square, bias=0.0, scale=1.0,
                            accum_out=S_[:, 80 + s * 2 + half:81 + s * 2 + half]),
                        deps=[tok_half[half]] + rmslock + glob["vout_lock"][c % N_VOUT]
                             + ([tsq[-1]] if tsq else []),
                        est=s_act(512, True)))
                glob.setdefault("vout_scr%d" % (c % N_VOUT), [])
                glob["vout_scr%d" % (c % N_VOUT)] += tsq
                tadd = sch.emit("vector", lambda e, s=s, S_=S_: e.tensor_tensor(
                    S_[:, 84 + s:85 + s], S_[:, 80 + s * 2:81 + s * 2],
                    S_[:, 81 + s * 2:82 + s * 2], ALU.add), deps=tsq, est=150)
                tlnr = sch.emit("scalar", lambda e, s=s, S_=S_: e.activation(
                    S_[:, 88 + s:89 + s], S_[:, 84 + s:85 + s], AF.Ln,
                    bias=RMS_EPS, scale=1.0 / D), deps=[tadd], est=250, afn="exp")
                trms = sch.emit("scalar", lambda e, s=s, S_=S_: e.activation(
                    S_[:, 86 + s:87 + s], S_[:, 88 + s:89 + s], AF.Exp, bias=0.0, scale=-0.5),
                    deps=[tlnr], est=250, afn="exp")
                evs = []
                for half in range(2):
                    evs.append(sch.emit(
                        "vector",
                        lambda e, b=ids[half], s=s, half=half, o2=o2, S_=S_: e.scalar_tensor_tensor(
                            tokv(o2)[:, s, half * 512:(half + 1) * 512], bank(b),
                            S_[:, 86 + s:87 + s], normw_rep[:, half * 512:(half + 1) * 512],
                            ALU.mult, ALU.mult,
                            accum_out=S_[:, 60 + s * 2 + half:61 + s * 2 + half]),
                        deps=[trms, tok_half[half]] + tok_nw + lock2 + (o2dep if s == 0 and half == 0 else []),
                        est=(120 + 512) / 0.96))
                tok_o2 += evs
                glob["stats_lock"][(ps, "rms%d" % (c % N_VOUT))] = [tlnr, trms] + evs
                banksM.give(ids, tsq + evs)
            p16.give(hF, all_ow)

            acc2 = stats[:, ps, 60:60 + 2 * SUB].rearrange("p (a b) -> p a b", b=2)
            t1b = sch.emit("vector", lambda e, S_=S_, acc2=acc2: e.tensor_tensor(
                S_[:, 16:16 + SUB], acc2[:, :, 0], acc2[:, :, 1], ALU.add),
                deps=tok_o2, est=150)
            trs2, S_, rd2 = ln_stats(tokv(o2), ps, 16, tok_o2, "b", s1_tok=[t1b], lock=lock2, bgrp=banksM,
                                     sub_deps=[tok_o2[0:2], tok_o2[2:4]])
            tok_n2 = []
            for s in range(SUB):
                tok_n2.append(sch.emit(
                    "vector",
                    lambda e, s=s, o2=o2, S_=S_: e.tensor_scalar(
                        tokv(o2)[:, s, :], tokv(o2)[:, s, :],
                        S_[:, 20 + s:21 + s], S_[:, 26 + s:27 + s],
                        ALU.subtract, ALU.mult),
                    deps=[trs2] + rd2, est=v_ts(D)))
            glob["stats_lock"][(ps, "b")] = tok_n2 + [t1b]
            lock3 = glob["stats_lock"].pop((ps, "c"), [])
            tok_x2l = []
            for s in range(SUB):
                tok_x2l.append(sch.emit(
                    "vector",
                    lambda e, s=s, xb=xb, o2=o2, S_=S_: e.scalar_tensor_tensor(
                        tokv(xb)[:, s, :], tokv(o2)[:, s, :], 1.0, tokv(xb)[:, s, :],
                        ALU.mult, ALU.add,
                        accum_out=S_[:, 32 + s:33 + s]),
                    deps=tok_n2 + stt["xb_readers"] + lock3, est=v_tt(1024)))
            tok_x2 = tok_x2l[-1]
            p32.give(o2, tok_x2l)
            stt["tok_x2"] = tok_x2l

            trs3, S_, rd3 = ln_stats(xt, ps, 32, tok_x2l, "c", s1_tok=tok_x2l, lock=lock3, bgrp=banksM,
                                     sub_deps=[[tok_x2l[0]], [tok_x2l[1]]])
            ln3, ln3dep = p16.take()
            tok_ln3 = []
            for s in range(SUB):
                tok_ln3.append(sch.emit(
                    "vector",
                    lambda e, s2=s, ln3=ln3, S_=S_, xt=xt: e.tensor_scalar(
                        tokv(ln3)[:, s2, :], xt[:, s2, :],
                        S_[:, 36 + s2:37 + s2], S_[:, 42 + s2:43 + s2],
                        ALU.subtract, ALU.mult),
                    deps=[trs3] + stt["tok_x2"] + ln3dep, est=v_ts(D)))
            glob["stats_lock"][(ps, "c")] = tok_ln3
            stt["x2_readers"] = rd3 + tok_ln3

            tok_evF = []
            tok_trF = []
            for s2 in range(SUB):
                tb = 0 if PT1 else (1 if PT_ROLE else s2)
                trs_ = []
                for j in range(KC):
                    trs_.append(sch.emit(
                        "tensor",
                        lambda e, tb=tb, j=j, s2=s2, ln3=ln3: e.transpose(
                            tbank(tb)[:, j * 128:(j + 1) * 128],
                            tokv(ln3)[:, s2, j * 128:(j + 1) * 128],
                            ident[:, :]),
                        deps=[tok_ln3[s2], tok_id] + tb_free[tb], est=65))
                tev = sch.emit(
                    "scalar",
                    lambda e, tb=tb, s2=s2: e.activation(
                        fF[:, 0:KC, 1 + s2 * 128: 1 + (s2 + 1) * 128],
                        tbank(tb), AF.Copy, bias=0.0, scale=1.0),
                    deps=trs_ + [glob["tok_tailF"]], est=s_act(D, True))
                tok_evF.append(tev)
                tok_trF += trs_
                tb_free[tb] = [tev]
            p16.give(ln3, tok_trF)
            stt["tok_evF"] = tok_evF

        def emit_B2b(c, stt):
            par = stt["par"]
            ps = stt["ps"]
            xb = stt["xb"]
            tok_evF = stt["tok_evF"]

            dbuf, ddep = p16.take()
            tok_d = sch.emit(
                "vector",
                lambda e, d=dbuf: e.tensor_tensor(
                    feat(d), fF[:, :, 1:CH + 1], fF[:, :, 0:CH], ALU.subtract),
                deps=tok_evF + [glob["tok_tailF"]] + ddep, est=v_tt(SUB * D, True))
            xrF, xrdep = p16.take()
            xkF, xkdep = p16.take()
            mur_bc = bass.AP(cols["mur"], 0, [[KC, 128], [1, KC], [0, CH]])
            muk_bc = bass.AP(cols["muk"], 0, [[KC, 128], [1, KC], [0, CH]])
            t_xr1 = sch.emit("vector", lambda e, d=dbuf, xr=xrF, bc=mur_bc: e.tensor_tensor(
                feat(xr), feat(d), bc, ALU.mult),
                deps=[tok_d, tok_dc] + xrdep, est=v_tt(SUB * D, True))
            tok_xr = [sch.emit("vector", lambda e, xr=xrF: e.tensor_tensor(
                feat(xr), feat(xr), fF[:, :, 0:CH], ALU.add),
                deps=[t_xr1], est=v_tt(SUB * D, True))]
            t_xk1 = sch.emit("vector", lambda e, d=dbuf, xk=xkF, bc=muk_bc: e.tensor_tensor(
                feat(xk), feat(d), bc, ALU.mult),
                deps=[tok_d] + xkdep, est=v_tt(SUB * D, True))
            tok_xk = [sch.emit("vector", lambda e, xk=xkF: e.tensor_tensor(
                feat(xk), feat(xk), fF[:, :, 0:CH], ALU.add),
                deps=[t_xk1], est=v_tt(SUB * D, True))]
            p16.give(dbuf, [t_xr1, t_xk1])
            glob["tok_tailF"] = sch.emit(
                "gpsimd", lambda e: e.tensor_copy(fF[:, :, 0:1], fF[:, :, CH:CH + 1]),
                deps=[tok_d] + tok_xr + tok_xk + [glob["tok_tailF"]], est=400)

            rf32, rfdep = (p16.take() if RF16 else p32.take())
            tok_rf = []
            all_wr = []
            for s in range(SUB):
                for half in range(2):
                    ids, bdeps = banksB.take(1)
                    b = ids[0]
                    tmm = None
                    for k in range(KC):
                        prev = [tmm] if tmm is not None else []
                        tmm = sch.emit(
                            "tensor",
                            lambda e, b=b, k=k, s=s, half=half, xr=xrF: e.matmul(
                                bank(b), feat(xr)[:, k, s * 128:(s + 1) * 128],
                                wsb["fr"][:, k, half * 512:(half + 1) * 512],
                                start=(k == 0), stop=(k == KC - 1)),
                            deps=tok_xr + wcast["fr"] + bdeps + prev, est=mm_est(512))
                    all_wr.append(tmm)
                    te = sch.emit(
                        "scalar",
                        lambda e, b=b, s=s, half=half, rf=rf32: e.activation(
                            tokv(rf)[:, s, half * 512:(half + 1) * 512], bank(b),
                            AF.Sigmoid, bias=0.0, scale=1.0),
                        deps=[tmm] + (rfdep if s == 0 and half == 0 else []),
                        est=s_act(512, True), afn="sigmoid")
                    tok_rf.append(te)
                    banksB.give(ids, [te])
            p16.give(xrF, all_wr)

            krelu, krdep = p16.take()
            tok_kr = []
            all_wk = []
            for j in range(KC // 2):
                ids, bdeps = banksB.take(1)
                b = ids[0]
                tmm = None
                pair_last = []
                for half in range(2):
                    m = 2 * j + half
                    for k in range(KC):
                        prev = [tmm] if tmm is not None else []
                        tmm = sch.emit(
                            "tensor",
                            lambda e, b=b, k=k, m=m, half=half, xk=xkF: e.matmul(
                                PS[:, b * 512 + half * CH: b * 512 + half * CH + CH],
                                wsb["fk"][:, k, m * 128:(m + 1) * 128],
                                feat(xk)[:, k, :],
                                start=(k == 0), stop=(k == KC - 1)),
                            deps=tok_xk + wcast["fk"] + bdeps + prev, est=mm_est(CH))
                    all_wk.append(tmm)
                    pair_last.append(tmm)
                te = sch.emit(
                    "scalar",
                    lambda e, b=b, j=j, kr=krelu: e.activation(
                        feat(kr)[:, 2 * j:2 * j + 2, :], bank(b, 2 * CH), AF.Relu,
                        bias=0.0, scale=1.0),
                    deps=pair_last + (krdep if j == 0 else []), est=s_act(2 * CH, True))
                tok_kr.append(te)
                banksB.give(ids, [te])
            p16.give(xkF, all_wk)
            kaF, kadep = p16.take()
            tok_ka = sch.emit("vector", lambda e, ka=kaF, kr=krelu: e.tensor_tensor(
                flat(ka), flat(kr), flat(kr), ALU.mult),
                deps=tok_kr + kadep, est=v_tt(SUB * D, True))
            p16.give(krelu, [tok_ka])

            lock4 = glob["stats_lock"].pop((ps, "d"), [])
            S_ = stats[:, ps, :]
            vout = stt["vout"]
            vscr = glob.pop("vout_scr%d" % (c % N_VOUT), [])
            tok_vo = []
            all_wv = []
            for s in range(SUB):
                for half in range(2):
                    ids, bdeps = banksB.take(1)
                    b = ids[0]
                    tmm = None
                    for k in range(KC):
                        prev = [tmm] if tmm is not None else []
                        tmm = sch.emit(
                            "tensor",
                            lambda e, b=b, k=k, s=s, half=half, ka=kaF: e.matmul(
                                bank(b), feat(ka)[:, k, s * 128:(s + 1) * 128],
                                wsb["fv"][:, k, half * 512:(half + 1) * 512],
                                start=(k == 0), stop=(k == KC - 1)),
                            deps=[tok_ka] + wcast["fv"] + bdeps + prev, est=mm_est(512))
                    all_wv.append(tmm)
                    te = sch.emit(
                        "vector",
                        lambda e, b=b, s=s, half=half, rf=rf32, S_=S_, vout=vout: e.scalar_tensor_tensor(
                            tokv(vout)[:, s, half * 512:(half + 1) * 512], bank(b),
                            1.0, tokv(rf)[:, s, half * 512:(half + 1) * 512],
                            ALU.mult, ALU.mult,
                            accum_out=S_[:, 64 + s * 2 + half:65 + s * 2 + half]),
                        deps=[tmm, tok_rf[s * 2 + half], ("DO", 16 * glob["nstores"])]
                             + glob["vout_lock"][c % N_VOUT] + vscr + lock4,
                        est=(120 + 512) / 0.96)
                    tok_vo.append(te)
                    banksB.give(ids, [te])
            p16.give(kaF, all_wv)
            (p16 if RF16 else p32).give(rf32, tok_vo)

            acc4 = stats[:, ps, 64:64 + 2 * SUB].rearrange("p (a b) -> p a b", b=2)
            t1d = sch.emit("vector", lambda e, S_=S_, acc4=acc4: e.tensor_tensor(
                S_[:, 48:48 + SUB], acc4[:, :, 0], acc4[:, :, 1], ALU.add),
                deps=tok_vo, est=150)
            trs4, S_, rd4 = ln_stats(tokv(vout), ps, 48, tok_vo, "d", s1_tok=[t1d], lock=lock4, bgrp=banksB,
                                     sub_deps=[tok_vo[0:2], tok_vo[2:4]])
            tok_n4 = []
            for s in range(SUB):
                tok_n4.append(sch.emit(
                    "vector",
                    lambda e, s=s, S_=S_, vout=vout: e.tensor_scalar(
                        tokv(vout)[:, s, :], tokv(vout)[:, s, :],
                        S_[:, 52 + s:53 + s], S_[:, 58 + s:59 + s],
                        ALU.subtract, ALU.mult),
                    deps=[trs4] + rd4, est=v_ts(D)))
            glob["stats_lock"][(ps, "d")] = tok_n4 + [t1d]
            tok_xf_l = []
            for s in range(SUB):
                tok_xf_l.append(sch.emit(
                    "gpsimd" if XFG else "vector",
                    lambda e, s=s, xb=xb, vout=vout: e.tensor_tensor(
                        tokv(xb)[:, s, :], tokv(xb)[:, s, :], tokv(vout)[:, s, :],
                        ALU.add),
                    deps=[tok_n4[s]] + stt["x2_readers"], est=3800 if XFG else v_tt(D)))
            glob["vout_lock"][c % N_VOUT] = list(tok_xf_l)
            for s in range(SUB):
                glob["nstores"] += 1
                sch.emit("sync", dma(x_chunk_ap(c, y_h)[:, s, :], tokv(xb)[:, s, :]),
                         deps=[tok_xf_l[s]], sem="DO", inc=16, est=2600)
            if c + NXIN < NCH:
                p2 = (c + NXIN) % NXIN
                glob["nloads"][p2] += 1
                sch.emit("sync", dma(tokv(xin[p2]), x_chunk_ap(c + NXIN, x_h)),
                         deps=[("DO", 16 * glob["nstores"])], sem="DX%d" % p2, inc=16,
                         est=5000)

        states = {}
        for r in range(NCH + 1):
            sF = emit_F_head(r) if r < NCH else None
            if r >= 1:
                emit_B1(r - 1, states[r - 1])
                emit_B2a(r - 1, states[r - 1])
            if r < NCH:
                emit_F_rest(r, sF)
                emit_F_tail(r, sF)
                states[r] = sF
            if r >= 1:
                emit_B2b(r - 1, states.pop(r - 1))

        sch.emit("sync", None, deps=[("DO", 16 * glob["nstores"])], est=100)

        streams = sch.schedule()
        t_needed = set()
        for op in sch.all:
            for d in op.deps:
                if isinstance(d, Op) and d.eng == "tensor" and op.eng != "tensor":
                    t_needed.add(d.idx)
        cnt = {k: 0 for k in ("T", "V", "S", "G")}
        dma_cnt = {}
        for e in ENGS:
            for op in streams[e]:
                if op.sem is not None:
                    dma_cnt[op.sem] = dma_cnt.get(op.sem, 0) + op.inc
                    op.inc_val = (op.sem, dma_cnt[op.sem])
                elif e in SELF_CHAIN:
                    own = ENGSEM[e]
                    cnt[own] += 1
                    op.inc_val = (own, cnt[own])
                elif e == "tensor" and op.idx in t_needed:
                    cnt["T"] += 1
                    op.inc_val = ("T", cnt["T"])

        with nc.Block() as block:
            def mk(engname):
                chain = engname in SELF_CHAIN

                def run(eng):
                    wcache = {}
                    for op in streams[engname]:
                        waits = []
                        for d in op.deps:
                            if isinstance(d, Op):
                                if d.eng == engname:
                                    continue
                                iv = d.inc_val
                                if iv is None:
                                    raise RuntimeError("dep without inc %d<-%d" % (op.idx, d.idx))
                                waits.append(iv)
                            else:
                                waits.append(d)
                        if chain and op.sem is None and op.inc_val is not None:
                            s, v = op.inc_val
                            waits.append((s, v - 1))
                        for s, v in waits:
                            if v > wcache.get(s, -1):
                                eng.wait_ge(sems[s], v)
                                wcache[s] = v
                        if op.fn is None:
                            continue
                        ins = op.fn(eng)
                        if op.inc_val is not None:
                            s, v = op.inc_val
                            ins.then_inc(sems[s], op.inc if op.sem else 1)
                return run

            block.tensor(mk("tensor"))
            block.vector(mk("vector"))
            block.scalar(mk("scalar"))
            block.gpsimd(mk("gpsimd"))
            block.sync(mk("sync"))

    return nc


_NC_CACHE = {}


def kernel(L0_attn_x, L0_attn_conv_weight, L0_attn_conv_bias, L0_attn_Wr_w,
           L0_attn_Wr_b, L0_attn_Wi_w, L0_attn_Wi_b, L0_attn_log_a,
           L0_attn_out_w, L0_attn_norm_w, L0_ffn_W_r, L0_ffn_W_k, L0_ffn_W_v,
           L0_ffn_mu_r, L0_ffn_mu_k):
    if "nc" not in _NC_CACHE:
        _NC_CACHE["nc"] = build_nc()
    nc = _NC_CACHE["nc"]

    x = np.asarray(L0_attn_x, dtype=np.float32)
    shared = {
        "wr": np.ascontiguousarray(np.asarray(L0_attn_Wr_w, dtype=np.float32).T),
        "wi": np.ascontiguousarray(np.asarray(L0_attn_Wi_w, dtype=np.float32).T),
        "ow": np.ascontiguousarray(np.asarray(L0_attn_out_w, dtype=np.float32).T),
        "fr": np.ascontiguousarray(np.asarray(L0_ffn_W_r, dtype=np.float32).T),
        "fk": np.ascontiguousarray(np.asarray(L0_ffn_W_k, dtype=np.float32).T),
        "fv": np.ascontiguousarray(np.asarray(L0_ffn_W_v, dtype=np.float32).T),
        "convw": np.ascontiguousarray(np.asarray(L0_attn_conv_weight, dtype=np.float32)[:, 0, :]),
        "convb": np.ascontiguousarray(np.asarray(L0_attn_conv_bias, dtype=np.float32)),
        "wrb": np.ascontiguousarray(np.asarray(L0_attn_Wr_b, dtype=np.float32)),
        "wib": np.ascontiguousarray(np.asarray(L0_attn_Wi_b, dtype=np.float32)),
        "loga": np.ascontiguousarray(np.asarray(L0_attn_log_a, dtype=np.float32)),
        "normw": np.ascontiguousarray(np.asarray(L0_attn_norm_w, dtype=np.float32)),
        "mur": np.ascontiguousarray(np.asarray(L0_ffn_mu_r, dtype=np.float32)),
        "muk": np.ascontiguousarray(np.asarray(L0_ffn_mu_k, dtype=np.float32)),
    }
    in_maps = []
    for b in range(8):
        m = dict(shared)
        m["x"] = np.ascontiguousarray(x[b])
        in_maps.append(m)
    res = run_bass_kernel_spmd(nc, in_maps, core_ids=list(range(8)))
    out = np.stack([np.asarray(r["y"]) for r in res.results], axis=0)
    return out.astype(np.float32)
